# version 12
# baseline (speedup 1.0000x reference)
import numpy as np
import ml_dtypes
from contextlib import ExitStack

import concourse.bass as bass
import concourse.mybir as mybir
from concourse.bass_utils import run_bass_kernel_spmd
from concourse.alu_op_type import AluOpType as OP

F32 = mybir.dt.float32
BF16 = mybir.dt.bfloat16
I32 = mybir.dt.int32
AF = mybir.ActivationFunctionType
AX = mybir.AxisListType

L = 4096
D = 1024
E = 2048
N = 16
DDL = 64
H = 3072
CH = 512
CH4 = 256
CHF = 256
ET = E // 128
DT = D // 128
HT = H // 128
EPS = 1e-5
FFS = 1.0 / 127.0

ENGS = ("sync", "vector", "scalar", "tensor", "gpsimd")

_BUILD_PASSES = (1, 2, 3, 4, 5)


class Sem:
    __slots__ = ("h", "n")

    def __init__(self, h):
        self.h = h
        self.n = 0


NSLOTS = 32


class Slot:
    __slots__ = ("h", "n")

    def __init__(self, h):
        self.h = h
        self.n = 0


class SemRef:
    __slots__ = ("slot", "base")

    def __init__(self, slot, base):
        self.slot = slot
        self.base = base

    @property
    def n(self):
        return self.slot.n - self.base


class K:
    def __init__(self, nc, stack):
        self.nc = nc
        self.stack = stack
        self.plan = {e: [] for e in ENGS}
        self.sems = {}
        self.g = {}
        self.slots = [Slot(stack.enter_context(nc.semaphore(f"slot{i}")))
                      for i in range(NSLOTS)]
        self.scope = None
        self.scope_used = 0

    def sem(self, name):
        if self.scope is not None and not name.startswith(("init", "bar")):
            if name not in self.scope:
                assert self.scope_used < NSLOTS, "out of sem slots"
                slot = self.slots[self.scope_used]
                self.scope[name] = SemRef(slot, slot.n)
                self.scope_used += 1
            return self.scope[name]
        if name not in self.sems:
            slot = Slot(self.stack.enter_context(self.nc.semaphore(name)))
            self.sems[name] = SemRef(slot, 0)
        return self.sems[name]

    def op(self, eng, build, waits=(), incs=()):
        w = [(s.slot.h, s.base + v) for (s, v) in waits if v > 0]
        pl = []
        for (s, amt) in incs:
            if amt <= 0:
                continue
            s.slot.n += amt
            pl.append((s.slot.h, amt))

        def fn(e):
            for (hh, v) in w:
                e.wait_ge(hh, v)
            ins = build(e)
            drained = False
            for idx, (hh, amt) in enumerate(pl):
                if ins is not None and idx == 0:
                    ins.then_inc(hh, amt)
                else:
                    if not drained and eng != "sync":
                        e.drain()
                        drained = True
                    e.sem_inc(hh, amt)
        self.plan[eng].append(fn)

    def barrier(self):
        b = self.sem("bar")
        for eng in ENGS:
            def bld(e, eng=eng):
                if eng != "sync":
                    e.drain()
                return None
            self.op(eng, bld, incs=[(b, 1)])
        t = b.n
        for eng in ENGS:
            self.op(eng, lambda e: None, waits=[(b, t)])

    def fresh(self, name):
        if name not in self.sems:
            slot = Slot(self.stack.enter_context(self.nc.semaphore(name)))
            self.sems[name] = SemRef(slot, 0)
        return self.sems[name]

    def begin_pass(self):
        self.scope = {}
        self.scope_used = 0

    def end_pass(self):
        self.scope = None
        self.barrier()


def emit_ln_stats(k, pre, rbuf, cols, sqscr, c, nlt, waits_first):
    s_rs = k.sem(pre + "_rs")
    s_rb_a = k.sem(pre + "_rb_a")
    for lt in range(nlt):
        k.op("scalar", lambda e, lt=lt: e.activation(
            out=sqscr[:], in_=rbuf[:, lt * D:(lt + 1) * D], func=AF.Identity,
            accum_out=cols[:, lt:lt + 1]),
            waits=list(waits_first) if lt == 0 else [])
    for lt in range(nlt):
        k.op("scalar", lambda e, lt=lt: e.activation(
            out=sqscr[:], in_=rbuf[:, lt * D:(lt + 1) * D], func=AF.Square,
            accum_out=cols[:, 4 + lt:5 + lt]))
    k.op("scalar", lambda e: e.activation(
        out=cols[:, 8:8 + nlt], in_=cols[:, 0:nlt], func=AF.Identity,
        scale=-1.0 / D))
    k.op("scalar", lambda e: e.activation(
        out=cols[:, 16:16 + nlt], in_=cols[:, 8:8 + nlt], func=AF.Square))
    k.op("scalar", lambda e: e.activation(
        out=cols[:, 20:20 + nlt], in_=cols[:, 16:16 + nlt], func=AF.Identity,
        scale=-1.0))
    k.op("scalar", lambda e: e.activation(
        out=cols[:, 24:24 + nlt], in_=cols[:, 4:4 + nlt], func=AF.Identity,
        scale=1.0 / D))
    for lt in range(nlt):
        k.op("scalar", lambda e, lt=lt: e.activation(
            out=cols[:, 28 + lt:29 + lt], in_=cols[:, 24 + lt:25 + lt],
            func=AF.Identity, bias=cols[:, 20 + lt:21 + lt]))
    k.op("scalar", lambda e: e.activation(
        out=cols[:, 28:28 + nlt], in_=cols[:, 28:28 + nlt], func=AF.Ln, bias=EPS))
    k.op("scalar", lambda e: e.activation(
        out=cols[:, 12:12 + nlt], in_=cols[:, 28:28 + nlt], func=AF.Exp, scale=-0.5),
        incs=[(s_rs, 1)])
    return s_rs, s_rb_a


def emit_transpose(k, pre, xnb, xt, ps_tr, c, nlt, waits_first):
    s_tr = k.sem(pre + "_tr")
    s_ev = k.sem(pre + "_ev")
    CL = nlt * 128
    for dt in range(DT):
        q = dt % 2
        g = DT * c + dt
        for lt in range(nlt):
            k.op("tensor", lambda e, dt=dt, lt=lt, q=q: e.transpose(
                ps_tr[:, q * 1024 + lt * 128:q * 1024 + (lt + 1) * 128],
                xnb[:, lt * D + dt * 128:lt * D + (dt + 1) * 128], k.g["ident"][:]),
                waits=(list(waits_first) if dt == 0 and lt == 0 else []) +
                      ([(s_ev, g - 1)] if g >= 2 and lt == 0 else []),
                incs=[(s_tr, 1)] if lt == nlt - 1 else [])
        k.op("vector", lambda e, dt=dt, q=q, xt=xt: e.tensor_copy(
            xt[:, dt * CL:(dt + 1) * CL], ps_tr[:, q * 1024:q * 1024 + CL]),
            waits=[(s_tr, g + 1)], incs=[(s_ev, 1)])
    return s_tr, s_ev



def emit_p1(k, ec, hb):
    nc = k.nc
    LT = CH // 128
    skipT = ec(nc.sbuf_tensor("p1_skipT", [128, DT * E], BF16))
    inT = ec(nc.sbuf_tensor("p1_inT", [128, DT * E], BF16))
    wd1T = ec(nc.sbuf_tensor("p1_wd1T", [128, ET * DDL], BF16))
    wbT = ec(nc.sbuf_tensor("p1_wbT", [128, ET * N], BF16))
    wcT = ec(nc.sbuf_tensor("p1_wcT", [128, ET * N], BF16))
    normw = ec(nc.sbuf_tensor("p1_normw", [128, DT], F32))
    convw = ec(nc.sbuf_tensor("p1_convw", [128, ET * 4], F32))
    convb = ec(nc.sbuf_tensor("p1_convb", [128, ET], F32))
    rbuf = [ec(nc.sbuf_tensor(f"p1_rb{i}", [128, LT * D], F32)) for i in range(2)]
    ln1 = ec(nc.sbuf_tensor("p1_ln1", [128, LT * D], F32))
    xnb = ec(nc.sbuf_tensor("p1_xnb", [128, LT * D], BF16))
    xnT = [ec(nc.sbuf_tensor("p1_xnT0", [128, DT * CH], BF16))]
    ext = ec(nc.sbuf_tensor("p1_ext", [128, ET * (CH + 3)], F32))
    cacc = [ec(nc.sbuf_tensor(f"p1_cacc{i}", [128, CH], F32)) for i in range(2)]
    ssb = [ec(nc.sbuf_tensor(f"p1_ssb{i}", [128, CH], F32)) for i in range(3)]
    xcb = [ec(nc.sbuf_tensor(f"p1_xcb{i}", [128, CH], F32)) for i in range(3)]
    xcbf = [ec(nc.sbuf_tensor(f"p1_xcbf{i}", [128, CH], BF16)) for i in range(3)]
    d1bf = ec(nc.sbuf_tensor("p1_d1bf", [DDL, CH], BF16))
    bmb = ec(nc.sbuf_tensor("p1_bmb", [N, CH], F32))
    cmb = ec(nc.sbuf_tensor("p1_cmb", [N, CH], F32))
    colsA = [ec(nc.sbuf_tensor(f"p1_cols{i}", [128, 48], F32)) for i in range(2)]

    ps_sk = [ec(nc.psum_tensor(f"p1_ps_sk{i}", [128, CH], F32)) for i in range(2)]
    ps_in1 = ec(nc.psum_tensor("p1_ps_in", [128, CH], F32))
    ps_tr = ec(nc.psum_tensor("p1_ps_tr", [128, 2048], BF16))
    ps_d1 = ec(nc.psum_tensor("p1_ps_d1", [DDL, CH], F32))
    ps_b = ec(nc.psum_tensor("p1_ps_b", [N, CH], F32))
    ps_c = ec(nc.psum_tensor("p1_ps_c", [N, CH], F32))

    s_w = k.fresh("p1_w")
    s_wv = k.sem("p1_wv")
    s_wf = k.sem("p1_wf")
    s_res = k.sem("p1_res")
    s_rb_v = k.sem("p1_rb_v")
    s_ln1 = k.sem("p1_ln1")
    s_sq2 = k.sem("p1_sq2")
    s_rs2 = k.sem("p1_rs2")
    s_xpl = k.sem("p1_xpl")
    s_xpl_st = k.sem("p1_xpl_st")
    s_xn = k.sem("p1_xn")
    s_mm_sk = k.sem("p1_mm_sk")
    s_mm_in = k.sem("p1_mm_in")
    s_ss_act = k.sem("p1_ss_act")
    s_ss_st = k.sem("p1_ss_st")
    s_evin = k.sem("p1_evin")
    s_conv = k.sem("p1_conv")
    s_xc_act = k.sem("p1_xc_act")
    s_xc_st = k.sem("p1_xc_st")
    s_xcbf = k.sem("p1_xcbf")
    s_dbc_i = k.sem("p1_dbc_i")
    s_dbc_ev = k.sem("p1_dbc_ev")
    s_dbc_st = k.sem("p1_dbc_st")

    for dt in range(DT):
        k.op("gpsimd", lambda e, dt=dt: e.dma_start(
            out=skipT[:, dt * E:(dt + 1) * E],
            in_=hb["skipT"][dt * 128:(dt + 1) * 128, :]), incs=[(s_w, 16)])
        k.op("gpsimd", lambda e, dt=dt: e.dma_start(
            out=inT[:, dt * E:(dt + 1) * E],
            in_=hb["inT"][dt * 128:(dt + 1) * 128, :]), incs=[(s_w, 16)])
    for et in range(ET):
        k.op("gpsimd", lambda e, et=et: e.dma_start(
            out=wd1T[:, et * DDL:(et + 1) * DDL],
            in_=hb["wd1T"][et * 128:(et + 1) * 128, :]), incs=[(s_w, 16)])
        k.op("gpsimd", lambda e, et=et: e.dma_start(
            out=wbT[:, et * N:(et + 1) * N],
            in_=hb["wbT"][et * 128:(et + 1) * 128, :]), incs=[(s_w, 16)])
        k.op("gpsimd", lambda e, et=et: e.dma_start(
            out=wcT[:, et * N:(et + 1) * N],
            in_=hb["wcT"][et * 128:(et + 1) * 128, :]), incs=[(s_w, 16)])
    for nm, t in (("normw_col", normw), ("convw_col", convw), ("convb_col", convb)):
        k.op("sync", lambda e, nm=nm, t=t: e.dma_start(out=t[:], in_=hb[nm][:]),
             incs=[(s_wv, 16)])
    wN, wvN = s_w.n, s_wv.n

    wstage = ec(nc.sbuf_tensor("p1_wstage", [128, E], BF16))
    for dt in range(DT):
        k.op("vector", lambda e, dt=dt: e.tensor_copy(
            wstage[:], skipT[:, dt * E:(dt + 1) * E]),
            waits=[(s_w, wN), (s_wv, wvN)] if dt == 0 else [])
        k.op("vector", lambda e, dt=dt: e.tensor_scalar(
            out=skipT[:, dt * E:(dt + 1) * E], in0=wstage[:],
            scalar1=normw[:, dt:dt + 1], scalar2=None, op0=OP.mult))
        k.op("vector", lambda e, dt=dt: e.tensor_copy(
            wstage[:], inT[:, dt * E:(dt + 1) * E]))
        k.op("vector", lambda e, dt=dt: e.tensor_scalar(
            out=inT[:, dt * E:(dt + 1) * E], in0=wstage[:],
            scalar1=normw[:, dt:dt + 1], scalar2=None, op0=OP.mult))
    for et in range(ET):
        k.op("vector", lambda e, et=et: e.memset(
            ext[:, et * (CH + 3):et * (CH + 3) + 3], 0.0),
            incs=[(s_wf, 1)] if et == ET - 1 else [])
    wfN = s_wf.n

    NCH = L // CH
    for c in range(NCH):
        cs = slice(c * CH, (c + 1) * CH)
        rb = rbuf[c % 2]
        xt = xnT[0]
        cols = colsA[c % 2]

        k.op("sync", lambda e, rb=rb, cs=cs: e.dma_start(
            out=rb[:].rearrange("p (lt d) -> p lt d", lt=LT),
            in_=hb["x0"][cs, :].rearrange("(lt p) d -> p lt d", p=128)),
            waits=([(k.sem("p1_rb_a"), c - 1)] if c >= 2 else []),
            incs=[(s_res, 16)])

        wf = [(s_res, 16 * (c + 1))] + ([(s_wf, wfN)] if c == 0 else [])
        s_rs, s_rb_a = emit_ln_stats(k, "p1", rb, cols, k.g["sqscr"], c, LT, wf)
        for lt in range(LT):
            k.op("scalar", lambda e, rb=rb, lt=lt, cols=cols: e.activation(
                out=ln1[:, lt * D:(lt + 1) * D], in_=rb[:, lt * D:(lt + 1) * D],
                func=AF.Identity, bias=cols[:, 8 + lt:9 + lt]),
                waits=([(s_sq2, c)] if lt == 0 and c >= 1 else []) +
                      ([(s_xpl_st, 16 * c)] if lt == 0 and c >= 1 else []))
            k.op("scalar", lambda e, lt=lt, cols=cols: e.activation(
                out=ln1[:, lt * D:(lt + 1) * D], in_=ln1[:, lt * D:(lt + 1) * D],
                func=AF.Identity, scale=cols[:, 12 + lt:13 + lt]),
                incs=([(s_ln1, 1), (s_xpl, 1), (s_rb_a, 1)] if lt == LT - 1 else []))
        k.op("sync", lambda e, cs=cs: e.dma_start(
            out=hb["xpl_s"][cs, :].rearrange("(lt p) d -> p lt d", p=128),
            in_=ln1[:].rearrange("p (lt d) -> p lt d", lt=LT)),
            waits=[(s_xpl, c + 1)], incs=[(s_xpl_st, 16)])
        for lt in range(LT):
            k.op("scalar", lambda e, lt=lt, cols=cols: e.activation(
                out=k.g["sqscr"][:], in_=ln1[:, lt * D:(lt + 1) * D], func=AF.Square,
                accum_out=cols[:, 32 + lt:33 + lt]))
        k.op("scalar", lambda e, cols=cols: e.activation(
            out=cols[:, 36:36 + LT], in_=cols[:, 32:32 + LT], func=AF.Ln,
            scale=1.0 / D, bias=EPS))
        k.op("scalar", lambda e, cols=cols: e.activation(
            out=cols[:, 36:36 + LT], in_=cols[:, 36:36 + LT], func=AF.Exp,
            scale=-0.5),
            incs=[(s_sq2, 1)])
        for lt in range(LT):
            k.op("scalar", lambda e, lt=lt, cols=cols: e.activation(
                out=xnb[:, lt * D:(lt + 1) * D], in_=ln1[:, lt * D:(lt + 1) * D],
                func=AF.Identity, scale=cols[:, 36 + lt:37 + lt]),
                waits=([(k.sem("p1_tr"), DT * c)] if lt == 0 and c >= 1 else []),
                incs=[(s_xn, 1)] if lt == LT - 1 else [])
        s_tr, s_ev = emit_transpose(k, "p1", xnb, xt, ps_tr, c, LT,
                                    [(s_xn, c + 1)])

        for et in range(ET):
            j = c * ET + et
            pk, pi = ps_sk[et % 2], ps_in1
            for dt in range(DT):
                k.op("tensor", lambda e, dt=dt, et=et, pk=pk, xt=xt: e.matmul(
                    pk[:], skipT[:, dt * E + et * 128:dt * E + (et + 1) * 128],
                    xt[:, dt * CH:(dt + 1) * CH],
                    start=(dt == 0), stop=(dt == DT - 1)),
                    waits=([(s_ev, DT * (c + 1))] if dt == 0 and et == 0 else []) +
                          ([(s_ss_act, j - 1)] if dt == 0 and j >= 2 else []),
                    incs=[(s_mm_sk, 1)] if dt == DT - 1 else [])
            for dt in range(DT):
                k.op("tensor", lambda e, dt=dt, et=et, pi=pi, xt=xt: e.matmul(
                    pi[:], inT[:, dt * E + et * 128:dt * E + (et + 1) * 128],
                    xt[:, dt * CH:(dt + 1) * CH],
                    start=(dt == 0), stop=(dt == DT - 1)),
                    waits=[(s_evin, j)] if dt == 0 and j >= 1 else [],
                    incs=[(s_mm_in, 1)] if dt == DT - 1 else [])
            sst = ssb[j % 3]
            k.op("scalar", lambda e, pk=pk, sst=sst: e.activation(
                out=sst[:], in_=pk[:], func=AF.Silu),
                waits=[(s_mm_sk, j + 1)] + ([(s_ss_st, 16 * (j - 2))] if j >= 3 else []),
                incs=[(s_ss_act, 1)])
            k.op("sync", lambda e, sst=sst, et=et, cs=cs: e.dma_start(
                out=hb["ss_s"][et * 128:(et + 1) * 128, cs], in_=sst[:]),
                waits=[(s_ss_act, j + 1)], incs=[(s_ss_st, 16)])
            eb = ext[:, et * (CH + 3):(et + 1) * (CH + 3)]
            k.op("vector", lambda e, pi=pi, eb=eb: e.tensor_copy(eb[:, 3:CH + 3], pi[:]),
                 waits=[(s_mm_in, j + 1)], incs=[(s_evin, 1)])
            ca, cb2 = cacc
            k.op("vector", lambda e, eb=eb, et=et, ca=ca: e.tensor_scalar(
                out=ca[:], in0=eb[:, 0:CH], scalar1=convw[:, et * 4:et * 4 + 1],
                scalar2=None, op0=OP.mult))
            k.op("vector", lambda e, eb=eb, et=et, ca=ca, cb2=cb2: e.scalar_tensor_tensor(
                out=cb2[:], in0=eb[:, 1:CH + 1], scalar=convw[:, et * 4 + 1:et * 4 + 2],
                in1=ca[:], op0=OP.mult, op1=OP.add))
            k.op("vector", lambda e, eb=eb, et=et, ca=ca, cb2=cb2: e.scalar_tensor_tensor(
                out=ca[:], in0=eb[:, 2:CH + 2], scalar=convw[:, et * 4 + 2:et * 4 + 3],
                in1=cb2[:], op0=OP.mult, op1=OP.add))
            k.op("vector", lambda e, eb=eb, et=et, ca=ca, cb2=cb2: e.scalar_tensor_tensor(
                out=cb2[:], in0=eb[:, 3:CH + 3], scalar=convw[:, et * 4 + 3:et * 4 + 4],
                in1=ca[:], op0=OP.mult, op1=OP.add))
            k.op("vector", lambda e, eb=eb: e.tensor_copy(eb[:, 0:3], eb[:, CH:CH + 3]),
                 incs=[(s_conv, 1)])
            xct = xcb[j % 3]
            k.op("scalar", lambda e, cb2=cb2, et=et, xct=xct: e.activation(
                out=xct[:], in_=cb2[:], func=AF.Silu, bias=convb[:, et:et + 1]),
                waits=[(s_conv, j + 1)] + ([(s_xc_st, 16 * (j - 2))] if j >= 3 else []),
                incs=[(s_xc_act, 1)])
            k.op("sync", lambda e, xct=xct, et=et, cs=cs: e.dma_start(
                out=hb["xc_s"][et * 128:(et + 1) * 128, cs], in_=xct[:]),
                waits=[(s_xc_act, j + 1)], incs=[(s_xc_st, 16)])
            xbf = xcbf[j % 3]
            k.op("vector", lambda e, xct=xct, xbf=xbf: e.tensor_copy(xbf[:], xct[:]),
                 waits=[(s_xc_act, j + 1)] + ([(s_dbc_i, j - 2)] if j >= 3 else []),
                 incs=[(s_xcbf, 1)])
            for et2 in ([et - 2] if et >= 2 else []) + \
                       ([ET - 2, ET - 1] if et == ET - 1 else []):
                j2 = c * ET + et2
                xb2 = xcbf[j2 % 3]
                k.op("tensor", lambda e, et2=et2, xb2=xb2: e.matmul(
                    ps_d1[:], wd1T[:, et2 * DDL:(et2 + 1) * DDL], xb2[:],
                    start=(et2 == 0), stop=(et2 == ET - 1)),
                    waits=[(s_xcbf, j2 + 1)] +
                          ([(s_dbc_ev, 3 * c)] if et2 == 0 and c >= 1 else []))
                k.op("tensor", lambda e, et2=et2, xb2=xb2: e.matmul(
                    ps_b[:], wbT[:, et2 * N:(et2 + 1) * N], xb2[:],
                    start=(et2 == 0), stop=(et2 == ET - 1)))
                k.op("tensor", lambda e, et2=et2, xb2=xb2: e.matmul(
                    ps_c[:], wcT[:, et2 * N:(et2 + 1) * N], xb2[:],
                    start=(et2 == 0), stop=(et2 == ET - 1)),
                    incs=[(s_dbc_i, 1)])
        s_bcb = k.fresh("p1_bcb")
        k.op("vector", lambda e: e.tensor_copy(d1bf[:], ps_d1[:]),
             waits=[(s_dbc_i, ET * (c + 1))] +
                   ([(s_dbc_st, 48 * c), (s_bcb, 32 * c)] if c >= 1 else []),
             incs=[(s_dbc_ev, 1)])
        k.op("vector", lambda e: e.tensor_copy(bmb[:], ps_b[:]), incs=[(s_dbc_ev, 1)])
        k.op("vector", lambda e: e.tensor_copy(cmb[:], ps_c[:]), incs=[(s_dbc_ev, 1)])
        k.op("sync", lambda e, cs=cs: e.dma_start(out=hb["d1_s"][:, cs], in_=d1bf[:]),
             waits=[(s_dbc_ev, 3 * c + 3)], incs=[(s_dbc_st, 16)])
        k.op("sync", lambda e, cs=cs: e.dma_start(out=hb["bm_s"][:, cs], in_=bmb[:]),
             incs=[(s_dbc_st, 16)])
        k.op("sync", lambda e, cs=cs: e.dma_start(out=hb["cm_s"][:, cs], in_=cmb[:]),
             incs=[(s_dbc_st, 16)])
        k.op("gpsimd", lambda e, cs=cs: e.dma_start(out=hb["bmb_s"][:, cs], in_=bmb[:]),
             waits=[(s_dbc_ev, 3 * c + 3)], incs=[(s_bcb, 16)])
        k.op("gpsimd", lambda e, cs=cs: e.dma_start(out=hb["cmb_s"][:, cs], in_=cmb[:]),
             incs=[(s_bcb, 16)])

    k.op("sync", lambda e: None, waits=[
        (s_xpl_st, s_xpl_st.n), (s_ss_st, s_ss_st.n), (s_xc_st, s_xc_st.n),
        (s_dbc_st, s_dbc_st.n)])
    k.op("gpsimd", lambda e: None, waits=[(k.fresh("p1_bcb"), k.fresh("p1_bcb").n)])



def emit_p2(k, ec, hb):
    nc = k.nc
    LT = CH // 128
    NSP = L // CH
    outT = ec(nc.sbuf_tensor("p2_outT", [128, ET * D], BF16))
    wd2T = ec(nc.sbuf_tensor("p2_wd2T", [DDL, E], BF16))
    wd2b = ec(nc.sbuf_tensor("p2_wd2b", [128, ET], F32))
    wdd = ec(nc.sbuf_tensor("p2_wD", [128, ET], F32))
    negA = ec(nc.sbuf_tensor("p2_negA", [128, 2 * ET * N], F32))
    brep = ec(nc.sbuf_tensor("p2_brep", [128, N * CH], BF16))
    crep = ec(nc.sbuf_tensor("p2_crep", [128, N * CH], BF16))
    d1b = [ec(nc.sbuf_tensor(f"p2_d1b{i}", [DDL, CH], BF16)) for i in range(2)]
    xplb = [ec(nc.sbuf_tensor("p2_xplb0", [128, LT * D], F32))]
    xcld = [ec(nc.sbuf_tensor(f"p2_xc{i}", [128, CH], F32)) for i in range(3)]
    ssld = [ec(nc.sbuf_tensor(f"p2_ss{i}", [128, CH], F32)) for i in range(3)]
    rr = [ec(nc.sbuf_tensor(f"p2_rr{i}", [128, CH], F32)) for i in range(2)]
    aa = [ec(nc.sbuf_tensor("p2_aa0", [128, CH], F32))]
    qq = [ec(nc.sbuf_tensor("p2_qq0", [128, CH], F32))]
    pp = [ec(nc.sbuf_tensor(f"p2_pp{i}", [128, CH], F32)) for i in range(2)]
    dtl = [ec(nc.sbuf_tensor(f"p2_dt{i}", [128, CH], F32)) for i in range(2)]
    dxl = [ec(nc.sbuf_tensor(f"p2_dx{i}", [128, CH], F32)) for i in range(2)]
    abar = [ec(nc.sbuf_tensor(f"p2_ab{i}", [128, CH], F32)) for i in range(4)]
    ubuf = [ec(nc.sbuf_tensor(f"p2_ub{i}", [128, CH], F32)) for i in range(4)]
    hsl = ec(nc.sbuf_tensor("p2_hsl", [128, N * CH], F32))
    hc2 = [ec(nc.sbuf_tensor(f"p2_hc{i}", [128, 2 * CH], BF16)) for i in range(2)]
    yf1 = [ec(nc.sbuf_tensor(f"p2_yf1{i}", [128, CH], F32)) for i in range(2)]
    yfT = ec(nc.sbuf_tensor("p2_yfT", [128, ET * CH], BF16))
    x1b = [ec(nc.sbuf_tensor(f"p2_x1b{i}", [128, D], F32)) for i in range(2)]

    ps_d2 = [ec(nc.psum_tensor(f"p2_ps_d2{i}", [128, CH], F32)) for i in range(2)]
    ps_y = [ec(nc.psum_tensor(f"p2_ps_y{i}", [128, CH], F32)) for i in range(2)]
    ps_o = [ec(nc.psum_tensor(f"p2_ps_o{i}", [128, D], F32)) for i in range(2)]

    s_w = k.fresh("p2_w")
    s_wv = k.sem("p2_wv")
    s_negA = k.sem("p2_negA")
    s_br = k.fresh("p2_br")
    s_cr = k.fresh("p2_cr")
    s_d1l = k.sem("p2_d1l")
    s_xpll = k.sem("p2_xpll")
    s_xza = k.fresh("p2_xza")
    s_xcl = k.sem("p2_xcl")
    s_ssl = k.sem("p2_ssl")
    s_mm_d2 = k.sem("p2_mm_d2")
    s_sp_rd = k.sem("p2_sp_rd")
    s_sp = k.sem("p2_sp")
    s_del = k.sem("p2_del")
    s_dx = k.sem("p2_dx")
    s_abar = k.sem("p2_abar")
    s_u = k.sem("p2_u")
    s_scan = k.sem("p2_scan")
    s_hc = k.sem("p2_hc")
    s_hcp = k.sem("p2_hcp")
    s_ymm = k.sem("p2_ymm")
    s_yf1 = k.sem("p2_yf1")
    s_yf2 = k.sem("p2_yf2")
    s_mm_out = k.sem("p2_mm_out")
    s_x1 = k.sem("p2_x1")
    s_x1_st = k.sem("p2_x1_st")

    for et in range(ET):
        k.op("gpsimd", lambda e, et=et: e.dma_start(
            out=outT[:, et * D:(et + 1) * D],
            in_=hb["outT"][et * 128:(et + 1) * 128, :]), incs=[(s_w, 16)])
    k.op("gpsimd", lambda e: e.dma_start(out=wd2T[:], in_=hb["wd2T"][:]),
         incs=[(s_w, 16)])
    for nm, t in (("wd2b_col", wd2b), ("wD_col", wdd)):
        k.op("sync", lambda e, nm=nm, t=t: e.dma_start(out=t[:], in_=hb[nm][:]),
             incs=[(s_wv, 16)])
    k.op("sync", lambda e: e.dma_start(out=negA[:, ET * N:], in_=hb["alog_col"][:]),
         incs=[(s_wv, 16)])
    wN, wvN = s_w.n, s_wv.n
    k.op("scalar", lambda e: e.activation(
        out=negA[:, ET * N:], in_=negA[:, ET * N:], func=AF.Exp),
        waits=[(s_wv, wvN)], incs=[(s_negA, 1)])
    k.op("scalar", lambda e: e.activation(
        out=negA[:, 0:ET * N], in_=negA[:, ET * N:], func=AF.Identity, scale=-1.0),
        incs=[(s_negA, 1)])

    def emit_bc_batch(sp):
        cs2 = slice(sp * CH, (sp + 1) * CH)
        for n in range(N):
            wc = []
            if sp >= 1:
                i_last = ((sp - 1) * ET + ET - 1) * N + n
                wc = [(s_hc, i_last // 2 + 1)]
            k.op("gpsimd", lambda e, n=n, cs2=cs2: e.dma_start(
                out=brep[:, n * CH:(n + 1) * CH].rearrange("p (o l) -> p o l", o=1),
                in_=hb["bmb_s"][n:n + 1, cs2].partition_broadcast(128)),
                incs=[(s_br, 16)])
            k.op("scalar", lambda e, n=n, cs2=cs2: e.dma_start(
                out=crep[:, n * CH:(n + 1) * CH].rearrange("p (o l) -> p o l", o=1),
                in_=hb["cmb_s"][n:n + 1, cs2].partition_broadcast(128)),
                waits=wc, incs=[(s_cr, 16)])

    emit_bc_batch(0)
    for s in range(NSP):
        cs = slice(s * CH, (s + 1) * CH)
        if s >= 1:
            ps_ = s - 1
            for lt in (range(LT) if globals().get("P2_STAGE", 99) >= 3 else []):
                m = LT * ps_ + lt
                k.op("sync", lambda e, ps_=ps_, lt=lt, m=m: e.dma_start(
                    out=hb["rs1_s"][ps_ * CH + lt * 128:ps_ * CH + (lt + 1) * 128, :],
                    in_=x1b[m % 2][:]),
                    waits=[(s_x1, m + 1)], incs=[(s_x1_st, 16)])

        k.op("sync", lambda e, cs=cs, s=s: e.dma_start(
            out=d1b[s % 2][:], in_=hb["d1_s"][:, cs]),
            waits=[(s_mm_d2, ET * (s - 1))] if s >= 2 else [], incs=[(s_d1l, 16)])
        k.op("sync", lambda e, cs=cs, s=s: e.dma_start(
            out=xplb[0][:].rearrange("p (lt d) -> p lt d", lt=LT),
            in_=hb["xpl_s"][cs, :].rearrange("(lt p) d -> p lt d", p=128)),
            waits=[(s_x1, LT * s)] if s >= 1 and globals().get("P2_STAGE", 99) >= 3 else [],
            incs=[(s_xpll, 16)])
        k.op("gpsimd", lambda e, cs=cs: e.dma_start(
            out=xplb[0][:].rearrange("p (lt d) -> p lt d", lt=LT),
            in_=hb["x0"][cs, :].rearrange("(lt p) d -> p lt d", p=128),
            accum_op=mybir.AluOpType.add),
            waits=[(s_xpll, 16 * (s + 1))],
            incs=[(s_xza, 16)])
        for et in range(ET):
            j = s * ET + et
            _full = globals().get("P2_STAGE", 99) >= 3
            k.op("sync", lambda e, et=et, cs=cs, j=j: e.dma_start(
                out=xcld[j % 3][:], in_=hb["xc_s"][et * 128:(et + 1) * 128, cs]),
                waits=[(s_yf1, j - 2)] if j >= 3 and _full else [], incs=[(s_xcl, 16)])
            k.op("sync", lambda e, et=et, cs=cs, j=j: e.dma_start(
                out=ssld[j % 3][:], in_=hb["ss_s"][et * 128:(et + 1) * 128, cs]),
                waits=[(s_yf2, j - 2)] if j >= 3 and _full else [], incs=[(s_ssl, 16)])
        def emit_d2(et):
            j = s * ET + et
            pd = ps_d2[et % 2]
            k.op("tensor", lambda e, et=et, pd=pd, s=s: e.matmul(
                pd[:], wd2T[:, et * 128:(et + 1) * 128], d1b[s % 2][:],
                start=True, stop=True),
                waits=[(s_d1l, 16 * (s + 1))] +
                      ([(s_w, wN)] if j == 0 else []) +
                      ([(s_sp_rd, j - 1)] if j >= 2 else []),
                incs=[(s_mm_d2, 1)])
        emit_d2(0)
        for et in range(ET):
            j = s * ET + et
            pd = ps_d2[et % 2]
            if et + 1 < ET:
                emit_d2(et + 1)
            r_, a_, q_, p_ = rr[et % 2], aa[0], qq[0], pp[et % 2]
            k.op("scalar", lambda e, pd=pd, et=et, r_=r_: e.activation(
                out=r_[:], in_=pd[:], func=AF.Relu, bias=wd2b[:, et:et + 1]),
                waits=[(s_mm_d2, j + 1)] + ([(s_del, j - 1)] if j >= 2 else []))
            k.op("scalar", lambda e, pd=pd, et=et, a_=a_: e.activation(
                out=a_[:], in_=pd[:], func=AF.Abs, bias=wd2b[:, et:et + 1]),
                incs=[(s_sp_rd, 1)])
            k.op("scalar", lambda e, a_=a_, q_=q_: e.activation(
                out=q_[:], in_=a_[:], func=AF.Exp, scale=-1.0))
            k.op("scalar", lambda e, q_=q_, p_=p_: e.activation(
                out=p_[:], in_=q_[:], func=AF.Ln, bias=1.0), incs=[(s_sp, 1)])
            dt_, dx_ = dtl[et % 2], dxl[et % 2]
            k.op("gpsimd", lambda e, r_=r_, p_=p_, dt_=dt_: e.tensor_tensor(
                out=dt_[:], in0=r_[:], in1=p_[:], op=OP.add),
                waits=[(s_sp, j + 1)] +
                      ([(s_abar, N * (j - 1))] if j >= 2 and
                       globals().get("P2_STAGE", 99) >= 2 else []),
                incs=[(s_del, 1)])
            k.op("gpsimd", lambda e, dt_=dt_, dx_=dx_, j=j: e.tensor_tensor(
                out=dx_[:], in0=dt_[:], in1=xcld[j % 3][:], op=OP.mult),
                waits=[(s_xcl, 16 * (j + 1))],
                incs=[(s_dx, 1)])
            if globals().get("P2_STAGE", 99) < 2:
                continue
            for n in range(N):
                i = j * N + n
                ab, ub = abar[i % 4], ubuf[i % 4]
                k.op("scalar", lambda e, dt_=dt_, et=et, n=n, ab=ab: e.activation(
                    out=ab[:], in_=dt_[:], func=AF.Exp,
                    scale=negA[:, et * N + n:et * N + n + 1]),
                    waits=[(s_del, j + 1)] +
                          ([(s_negA, 2)] if i == 0 else []) +
                          ([(s_scan, i - 3)] if i >= 4 else []),
                    incs=[(s_abar, 1)])
                k.op("gpsimd", lambda e, dx_=dx_, n=n, ub=ub: e.tensor_tensor(
                    out=ub[:], in0=dx_[:], in1=brep[:, n * CH:(n + 1) * CH],
                    op=OP.mult),
                    waits=[(s_dx, j + 1), (s_br, 16 * (16 * s + n + 1))] +
                          ([(s_scan, i - 3)] if i >= 4 else []),
                    incs=[(s_u, 1)])
                k.op("vector", lambda e, ab=ab, ub=ub, n=n, et=et: e.tensor_tensor_scan(
                    hsl[:, n * CH:(n + 1) * CH], ab[:], ub[:],
                    k.g["state"][:, et * N + n:et * N + n + 1], OP.mult, OP.add),
                    waits=[(s_abar, i + 1), (s_u, i + 1)],
                    incs=[(s_scan, 1)])
                if n % 2 == 1:
                    q = i // 2
                    hcp = hc2[q % 2]
                    k.op("vector", lambda e, n=n, hcp=hcp: e.tensor_tensor(
                        out=hcp[:], in0=hsl[:, (n - 1) * CH:(n + 1) * CH],
                        in1=crep[:, (n - 1) * CH:(n + 1) * CH], op=OP.mult),
                        waits=[(s_cr, 16 * (16 * s + n + 1))] +
                              ([(s_ymm, 2 * q - 2)] if q >= 2 else []),
                        incs=[(s_hc, 1)])
                hca = hc2[(i // 2) % 2][:, (n % 2) * CH:(n % 2 + 1) * CH]
                k.op("tensor", lambda e, hca=hca, et=et, n=n: e.matmul(
                    ps_y[et % 2][:], k.g["ident"][:], hca[:],
                    start=(n == 0), stop=(n == N - 1)),
                    waits=[(s_hc, i // 2 + 1)] +
                          ([(s_yf1, j - 1)] if n == 0 and j >= 2 and
                           globals().get("P2_STAGE", 99) >= 3 else []),
                    incs=[(s_ymm, 1)])
            k.op("vector", lambda e, et=et: e.tensor_copy(
                k.g["state"][:, et * N:(et + 1) * N],
                hsl[:, CH - 1::CH]),
                )
            if globals().get("P2_STAGE", 99) < 2.5:
                continue
            yb = yf1[et % 2]
            k.op("vector", lambda e, et=et, j=j, yb=yb: e.scalar_tensor_tensor(
                out=yb[:], in0=xcld[j % 3][:], scalar=wdd[:, et:et + 1],
                in1=ps_y[et % 2][:], op0=OP.mult, op1=OP.add),
                waits=[(s_ymm, (j + 1) * N)], incs=[(s_yf1, 1)])
            k.op("vector", lambda e, et=et, j=j, yb=yb: e.tensor_tensor(
                out=yfT[:, et * CH:(et + 1) * CH], in0=yb[:], in1=ssld[j % 3][:],
                op=OP.mult),
                waits=[(s_yf1, j + 1), (s_ssl, 16 * (j + 1))] +
                      ([(s_mm_out, 2 * LT * s)] if et == 0 and s >= 1 and
                       globals().get("P2_STAGE", 99) >= 3 else []),
                incs=[(s_yf2, 1)])

        if s + 1 < NSP:
            emit_bc_batch(s + 1)

        for lt in (range(LT) if globals().get("P2_STAGE", 99) >= 3 else []):
            m = LT * s + lt
            po = ps_o[lt % 2]
            for et in range(ET):
                for half in range(2):
                    k.op("tensor", lambda e, lt=lt, et=et, half=half, po=po: e.matmul(
                        po[:, half * 512:(half + 1) * 512],
                        yfT[:, et * CH + lt * 128:et * CH + (lt + 1) * 128],
                        outT[:, et * D + half * 512:et * D + (half + 1) * 512],
                        start=(et == 0), stop=(et == ET - 1)),
                        waits=([(s_yf2, ET * (s + 1))] if et == 0 and half == 0 else []) +
                              ([(s_x1, m - 1)] if et == 0 and half == 0 and m >= 2 else []),
                        incs=[(s_mm_out, 1)] if et == ET - 1 else [])
            k.op("vector", lambda e, lt=lt, s=s, po=po, m=m: e.scalar_tensor_tensor(
                out=x1b[m % 2][:], in0=po[:], scalar=0.0,
                in1=xplb[0][:, lt * D:(lt + 1) * D], op0=OP.bypass, op1=OP.add),
                waits=[(s_mm_out, 2 * m + 2), (s_xza, 16 * (s + 1))] +
                      ([(s_x1_st, 16 * (m - 1))] if m >= 2 else []),
                incs=[(s_x1, 1)])
    s = NSP - 1
    for lt in (range(LT) if globals().get("P2_STAGE", 99) >= 3 else []):
        m = LT * s + lt
        k.op("sync", lambda e, s=s, lt=lt, m=m: e.dma_start(
            out=hb["rs1_s"][s * CH + lt * 128:s * CH + (lt + 1) * 128, :],
            in_=x1b[m % 2][:]),
            waits=[(s_x1, m + 1)], incs=[(s_x1_st, 16)])
    k.op("sync", lambda e: None, waits=[(s_x1_st, s_x1_st.n)])



def emit_ffn(k, ec, hb, tag, rs_in, rs_out):
    nc = k.nc
    LTF = CHF // 128
    NCHF = L // CHF
    wg = ec(nc.sbuf_tensor(f"{tag}_wg", [128, DT * H], BF16))
    wu = ec(nc.sbuf_tensor(f"{tag}_wu", [128, DT * H], BF16))
    wdt = [ec(nc.sbuf_tensor(f"{tag}_wd{i}", [128, D], BF16)) for i in range(6)]
    rbuf = [ec(nc.sbuf_tensor(f"{tag}_rb{i}", [128, LTF * D], F32)) for i in range(4)]
    xnb = ec(nc.sbuf_tensor(f"{tag}_xnb", [128, LTF * D], BF16))
    xnf32 = ec(nc.sbuf_tensor(f"{tag}_xnf32", [128, LTF * D], F32))
    xnT = [ec(nc.sbuf_tensor(f"{tag}_xnT{i}", [128, DT * CHF], BF16)) for i in range(2)]
    sg = [ec(nc.sbuf_tensor(f"{tag}_sg{i}", [128, CHF], F32)) for i in range(3)]
    hT = [ec(nc.sbuf_tensor(f"{tag}_hT{i}", [128, CHF], BF16)) for i in range(3)]
    ob = [ec(nc.sbuf_tensor(f"{tag}_ob{i}", [128, LTF * D], F32)) for i in range(2)]
    colsA = [ec(nc.sbuf_tensor(f"{tag}_cols{i}", [128, 48], F32)) for i in range(2)]

    ps_g = ec(nc.psum_tensor(f"{tag}_ps_g", [128, 512], F32))
    ps_u = ec(nc.psum_tensor(f"{tag}_ps_u", [128, 512], F32))
    ps_o = [ec(nc.psum_tensor(f"{tag}_ps_o{i}", [128, D], F32)) for i in range(2)]
    ps_tr = ec(nc.psum_tensor(f"{tag}_ps_tr", [128, 2048], BF16))

    s_w = k.fresh(tag + "_w")
    s_wd = k.fresh(tag + "_wd")
    s_res = k.sem(tag + "_res")
    s_rb_done = k.sem(tag + "_rbd")
    s_xn = k.sem(tag + "_xn")
    s_mm_g = k.sem(tag + "_mm_g")
    s_mm_u = k.sem(tag + "_mm_u")
    s_sg = k.sem(tag + "_sg")
    s_hT = k.sem(tag + "_hT")
    s_dn_i = k.sem(tag + "_dn_i")
    s_ob = k.sem(tag + "_ob")
    s_ob_st = k.sem(tag + "_ob_st")

    for dt in range(DT):
        k.op("gpsimd", lambda e, dt=dt: e.dma_start(
            out=wg[:, dt * H:(dt + 1) * H],
            in_=hb["wgT"][dt * 128:(dt + 1) * 128, :]), incs=[(s_w, 16)])
        k.op("gpsimd", lambda e, dt=dt: e.dma_start(
            out=wu[:, dt * H:(dt + 1) * H],
            in_=hb["wuT"][dt * 128:(dt + 1) * 128, :]), incs=[(s_w, 16)])
    wN = s_w.n

    def emit_rb_load(c, w):
        k.op("sync", lambda e, c=c: e.dma_start(
            out=rbuf[c % 4][:].rearrange("p (lt d) -> p lt d", lt=LTF),
            in_=rs_in[c * CHF:(c + 1) * CHF, :].rearrange("(lt p) d -> p lt d", p=128)),
            waits=w, incs=[(s_res, 16)])

    for c in range(min(3, NCHF)):
        emit_rb_load(c, [])
    for c in range(NCHF):
        cs = slice(c * CHF, (c + 1) * CHF)
        rb = rbuf[c % 4]
        xt = xnT[c % 2]
        cols = colsA[c % 2]
        if c + 3 < NCHF:
            emit_rb_load(c + 3, [(s_rb_done, c)])
        for ht in range(HT):
            jw = c * HT + ht
            k.op("gpsimd", lambda e, ht=ht, jw=jw: e.dma_start(
                out=wdt[jw % 6][:], in_=hb["wdT"][ht * 128:(ht + 1) * 128, :]),
                waits=[(s_dn_i, jw - 5)] if jw >= 6 else [],
                incs=[(s_wd, 16)])
        s_rs, s_rb_a = emit_ln_stats(
            k, tag, rb, cols, k.g["sqscr"], c, LTF, [(s_res, 16 * (c + 1))])
        for lt in range(LTF):
            k.op("scalar", lambda e, rb=rb, lt=lt, cols=cols: e.activation(
                out=xnf32[:, lt * D:(lt + 1) * D], in_=rb[:, lt * D:(lt + 1) * D],
                func=AF.Identity, bias=cols[:, 8 + lt:9 + lt]))
            k.op("scalar", lambda e, lt=lt, cols=cols: e.activation(
                out=xnb[:, lt * D:(lt + 1) * D], in_=xnf32[:, lt * D:(lt + 1) * D],
                func=AF.Identity, scale=cols[:, 12 + lt:13 + lt]),
                waits=([(k.sem(tag + "_tr"), DT * c)] if lt == 0 and c >= 1 else []),
                incs=([(s_xn, 1), (s_rb_a, 1)] if lt == LTF - 1 else []))
        s_tr, s_ev = emit_transpose(
            k, tag, xnb, xt, ps_tr, c, LTF,
            [(s_xn, c + 1)] + ([(s_w, wN)] if c == 0 else []))
        def emit_down(ht):
            j = c * HT + ht
            for lt in range(LTF):
                for half in range(2):
                    k.op("tensor", lambda e, ht=ht, lt=lt, half=half, j=j: e.matmul(
                        ps_o[lt][:, half * 512:(half + 1) * 512],
                        hT[j % 3][:, lt * 128:(lt + 1) * 128],
                        wdt[j % 6][:, half * 512:(half + 1) * 512],
                        start=(ht == 0), stop=(ht == HT - 1)),
                        waits=([(s_hT, j + 1), (s_wd, 16 * (j + 1))]
                               if lt == 0 and half == 0 else []) +
                              ([(s_ob, LTF * c)] if ht == 0 and lt == 0 and half == 0
                               and c >= 1 else []),
                        incs=[(s_dn_i, 1)] if lt == LTF - 1 and half == 1 else [])

        for ht in range(HT):
            j = c * HT + ht
            for dt in range(DT):
                k.op("tensor", lambda e, dt=dt, ht=ht, xt=xt: e.matmul(
                    ps_g[:, 0:CHF], wg[:, dt * H + ht * 128:dt * H + (ht + 1) * 128],
                    xt[:, dt * CHF:(dt + 1) * CHF],
                    start=(dt == 0), stop=(dt == DT - 1)),
                    waits=([(s_ev, DT * (c + 1))] if dt == 0 and ht == 0 else []) +
                          ([(s_sg, j)] if dt == 0 and j >= 1 else []),
                    incs=[(s_mm_g, 1)] if dt == DT - 1 else [])
            for dt in range(DT):
                k.op("tensor", lambda e, dt=dt, ht=ht, xt=xt: e.matmul(
                    ps_u[:, 0:CHF], wu[:, dt * H + ht * 128:dt * H + (ht + 1) * 128],
                    xt[:, dt * CHF:(dt + 1) * CHF],
                    start=(dt == 0), stop=(dt == DT - 1)),
                    waits=[(s_hT, j)] if dt == 0 and j >= 1 else [],
                    incs=[(s_mm_u, 1)] if dt == DT - 1 else [])
            k.op("scalar", lambda e, j=j: e.activation(
                out=sg[j % 3][:], in_=ps_g[:, 0:CHF], func=AF.Silu, scale=FFS),
                waits=[(s_mm_g, j + 1)], incs=[(s_sg, 1)])
            k.op("vector", lambda e, j=j: e.scalar_tensor_tensor(
                out=hT[j % 3][:], in0=ps_u[:, 0:CHF], scalar=FFS, in1=sg[j % 3][:],
                op0=OP.mult, op1=OP.mult),
                waits=[(s_mm_u, j + 1), (s_sg, j + 1)], incs=[(s_hT, 1)])
            if ht >= 1:
                emit_down(ht - 1)
        emit_down(HT - 1)
        for lt in range(LTF):
            k.op("vector", lambda e, lt=lt, rb=rb, c=c: e.scalar_tensor_tensor(
                out=ob[c % 2][:, lt * D:(lt + 1) * D], in0=ps_o[lt][:], scalar=FFS,
                in1=rb[:, lt * D:(lt + 1) * D], op0=OP.mult, op1=OP.add),
                waits=[(s_dn_i, HT * (c + 1))] +
                      ([(s_ob_st, 16 * (c - 1))] if lt == 0 and c >= 2 else []),
                incs=[(s_ob, 1)] + ([(s_rb_done, 1)] if lt == LTF - 1 else []))
        k.op("sync", lambda e, c=c, cs=cs: e.dma_start(
            out=rs_out[cs, :].rearrange("(lt p) d -> p lt d", p=128),
            in_=ob[c % 2][:].rearrange("p (lt d) -> p lt d", lt=LTF)),
            waits=[(s_ob, LTF * (c + 1))], incs=[(s_ob_st, 16)])
    k.op("sync", lambda e: None, waits=[(s_ob_st, s_ob_st.n)])



def emit_p4(k, ec, hb):
    nc = k.nc
    LT4 = CH4 // 128
    NC4 = L // CH4
    wk = ec(nc.sbuf_tensor("p4_wk", [128, DT * D], BF16))
    wka = ec(nc.sbuf_tensor("p4_wka", [128, DT * D], BF16))
    wv = ec(nc.sbuf_tensor("p4_wv", [128, DT * D], BF16))
    gcol = ec(nc.sbuf_tensor("p4_g", [128, DT], F32))
    bcol = ec(nc.sbuf_tensor("p4_b", [128, DT], F32))
    bcol_bf = ec(nc.sbuf_tensor("p4_bbf", [128, DT], BF16))
    bkab = ec(nc.sbuf_tensor("p4_bka", [128, 2 * DT], F32))
    bkrow = ec(nc.sbuf_tensor("p4_bkrow", [1, D], F32))
    bvrow = ec(nc.sbuf_tensor("p4_bvrow", [1, D], F32))
    biask = ec(nc.sbuf_tensor("p4_biask", [1, D], BF16))
    biasv = ec(nc.sbuf_tensor("p4_biasv", [1, D], BF16))
    ones_bf = ec(nc.sbuf_tensor("p4_ones", [1, CH4], BF16))
    ones_f = ec(nc.sbuf_tensor("p4_onesf", [128, CH4], F32))
    rbuf = [ec(nc.sbuf_tensor(f"p4_rb{i}", [128, LT4 * D], F32)) for i in range(2)]
    xnb = ec(nc.sbuf_tensor("p4_xnb", [128, LT4 * D], BF16))
    xnf32 = ec(nc.sbuf_tensor("p4_xnf32", [128, LT4 * D], F32))
    xnT = [ec(nc.sbuf_tensor(f"p4_xnT{i}", [128, DT * CH4], BF16)) for i in range(2)]
    keysT = [ec(nc.sbuf_tensor(f"p4_keys{i}", [128, DT * CH4], F32)) for i in range(2)]
    keysbf = ec(nc.sbuf_tensor("p4_keysbf", [128, DT * CH4], BF16))
    valsT = [ec(nc.sbuf_tensor(f"p4_vals{i}", [128, DT * CH4], F32)) for i in range(2)]
    kwT = ec(nc.sbuf_tensor("p4_kwT", [128, DT * CH4], F32))
    sT = ec(nc.sbuf_tensor("p4_sT", [128, DT * CH4], F32))
    ivT = ec(nc.sbuf_tensor("p4_ivT", [128, DT * CH4], F32))
    t1T = ec(nc.sbuf_tensor("p4_t1T", [128, DT * CH4], F32))
    kkT = ec(nc.sbuf_tensor("p4_kkT", [128, DT * CH4], F32))
    pT = ec(nc.sbuf_tensor("p4_pT", [128, DT * CH4], F32))
    x3b = [ec(nc.sbuf_tensor(f"p4_x3{i}", [128, LT4 * D], F32)) for i in range(2)]
    colsA = [ec(nc.sbuf_tensor(f"p4_cols{i}", [128, 48], F32)) for i in range(2)]

    ps_k = ec(nc.psum_tensor("p4_ps_k", [128, 512], F32))
    ps_ka = ec(nc.psum_tensor("p4_ps_ka", [128, 512], F32))
    ps_v = ec(nc.psum_tensor("p4_ps_v", [128, 512], F32))
    ps_ot = ec(nc.psum_tensor("p4_ps_ot", [128, D], F32))
    ps_tr = ec(nc.psum_tensor("p4_ps_tr", [128, 2048], BF16))

    s_w = k.fresh("p4_w")
    s_wv = k.sem("p4_wv")
    s_fold = k.sem("p4_fold")
    s_bias = k.sem("p4_bias")
    s_res = k.sem("p4_res")
    s_rb_v = k.sem("p4_rb_v")
    s_xn = k.sem("p4_xn")
    s_mm_k = k.sem("p4_mm_k")
    s_kev = k.sem("p4_kev")
    s_kbf = k.sem("p4_kbf")
    s_mm_ka = k.sem("p4_mm_ka")
    s_kw = k.sem("p4_kw")
    s_mm_v = k.sem("p4_mm_v")
    s_vev = k.sem("p4_vev")
    s_scs = k.sem("p4_scs")
    s_lnS = k.sem("p4_lnS")
    s_t1 = k.sem("p4_t1")
    s_kk = k.sem("p4_kk")
    s_pcs = k.sem("p4_pcs")
    s_at = k.sem("p4_at")
    s_otr = k.sem("p4_otr")
    s_x3 = k.sem("p4_x3")
    s_x3_st = k.sem("p4_x3_st")

    for dt in range(DT):
        k.op("gpsimd", lambda e, dt=dt: e.dma_start(
            out=wk[:, dt * D:(dt + 1) * D],
            in_=hb["wkT"][dt * 128:(dt + 1) * 128, :]), incs=[(s_w, 16)])
        k.op("gpsimd", lambda e, dt=dt: e.dma_start(
            out=wka[:, dt * D:(dt + 1) * D],
            in_=hb["wkaT"][dt * 128:(dt + 1) * 128, :]), incs=[(s_w, 16)])
        k.op("gpsimd", lambda e, dt=dt: e.dma_start(
            out=wv[:, dt * D:(dt + 1) * D],
            in_=hb["wvT"][dt * 128:(dt + 1) * 128, :]), incs=[(s_w, 16)])
    for nm, t in (("gamma_col", gcol), ("beta_col", bcol),
                  ("bk_row", bkrow), ("bv_row", bvrow)):
        k.op("sync", lambda e, nm=nm, t=t: e.dma_start(out=t[:], in_=hb[nm][:]),
             incs=[(s_wv, 16)])
    k.op("sync", lambda e: e.dma_start(out=bkab[:, 0:DT], in_=hb["bka_col"][:]),
         incs=[(s_wv, 16)])
    wN, wvN = s_w.n, s_wv.n
    k.op("vector", lambda e: e.memset(ones_bf[:], 1.0))
    k.op("vector", lambda e: e.memset(ones_f[:], 1.0))
    k.op("vector", lambda e: e.tensor_copy(bcol_bf[:], bcol[:]),
         waits=[(s_wv, wvN)], incs=[(s_fold, 1)])
    k.op("scalar", lambda e: e.activation(
        out=bkab[:, DT:2 * DT], in_=bkab[:, 0:DT], func=AF.Identity,
        scale=1.0 / 32.0),
        waits=[(s_wv, wvN)],
        incs=[(s_fold, 1)])
    foldN = s_fold.n
    from contextlib import ExitStack as _ES
    _bias_scope = _ES()
    ps_bias = _bias_scope.enter_context(nc.psum_tensor("p4_ps_bias", [1, 512], F32))
    rounds = [(wk, bkrow, biask, 0), (wk, bkrow, biask, 1),
              (wv, bvrow, biasv, 0), (wv, bvrow, biasv, 1)]
    for ri, (wt, brow, bout, half) in enumerate(rounds):
        for dt in range(DT):
            k.op("tensor", lambda e, wt=wt, half=half, dt=dt: e.matmul(
                ps_bias[0:1, 0:512],
                bcol_bf[:, dt:dt + 1],
                wt[:, dt * D + half * 512:dt * D + (half + 1) * 512],
                start=(dt == 0), stop=(dt == DT - 1)),
                waits=([(s_fold, foldN), (s_w, wN)] if ri == 0 and dt == 0
                       else []) +
                      ([(s_bias, 2 * ri)] if ri > 0 and dt == 0 else []),
                incs=[(s_bias, 1)] if dt == DT - 1 else [])
        tb = s_bias.n
        k.op("vector", lambda e, brow=brow, bout=bout, half=half: e.tensor_tensor(
            out=bout[0:1, half * 512:(half + 1) * 512], in0=ps_bias[0:1, 0:512],
            in1=brow[0:1, half * 512:(half + 1) * 512], op=OP.add),
            waits=[(s_bias, tb)], incs=[(s_bias, 1)])
    biasN = s_bias.n
    _bias_scope.close()
    wstage4 = ec(nc.sbuf_tensor("p4_wstage", [128, D], BF16))
    s_gf = k.sem("p4_gf")
    for dt in range(DT):
        k.op("vector", lambda e, dt=dt: e.tensor_copy(
            wstage4[:], wk[:, dt * D:(dt + 1) * D]),
            waits=[(s_bias, biasN)] if dt == 0 else [])
        k.op("vector", lambda e, dt=dt: e.tensor_scalar(
            out=wk[:, dt * D:(dt + 1) * D], in0=wstage4[:],
            scalar1=gcol[:, dt:dt + 1], scalar2=None, op0=OP.mult))
        k.op("vector", lambda e, dt=dt: e.tensor_copy(
            wstage4[:], wv[:, dt * D:(dt + 1) * D]))
        k.op("vector", lambda e, dt=dt: e.tensor_scalar(
            out=wv[:, dt * D:(dt + 1) * D], in0=wstage4[:],
            scalar1=gcol[:, dt:dt + 1], scalar2=None, op0=OP.mult),
            incs=[(s_gf, 1)] if dt == DT - 1 else [])
    gfN = s_gf.n

    for c in range(NC4):
        cs = slice(c * CH4, (c + 1) * CH4)
        rb = rbuf[c % 2]
        xt = xnT[c % 2]
        kt = keysT[c % 2]
        vt = valsT[c % 2]
        cols = colsA[c % 2]
        k.op("sync", lambda e, rb=rb, cs=cs: e.dma_start(
            out=rb[:].rearrange("p (lt d) -> p lt d", lt=LT4),
            in_=hb["rs2_s"][cs, :].rearrange("(lt p) d -> p lt d", p=128)),
            waits=([(s_rb_v, c - 1)] if c >= 2 and globals().get("P4_STAGE", 99) >= 4 else []),
            incs=[(s_res, 16)])
        s_rs, s_rb_a = emit_ln_stats(
            k, "p4", rb, cols, k.g["sqscr"], c, LT4, [(s_res, 16 * (c + 1))])
        for lt in range(LT4):
            k.op("scalar", lambda e, rb=rb, lt=lt, cols=cols: e.activation(
                out=xnf32[:, lt * D:(lt + 1) * D], in_=rb[:, lt * D:(lt + 1) * D],
                func=AF.Identity, bias=cols[:, 8 + lt:9 + lt]))
            k.op("scalar", lambda e, lt=lt, cols=cols: e.activation(
                out=xnf32[:, lt * D:(lt + 1) * D],
                in_=xnf32[:, lt * D:(lt + 1) * D],
                func=AF.Identity, scale=cols[:, 12 + lt:13 + lt]))
        for lt in range(LT4):
            k.op("scalar", lambda e, lt=lt, cols=cols: e.activation(
                out=k.g["sqscr"][:], in_=xnf32[:, lt * D:(lt + 1) * D],
                func=AF.Square, accum_out=cols[:, 32 + lt:33 + lt]))
        k.op("scalar", lambda e, cols=cols: e.activation(
            out=cols[:, 36:36 + LT4], in_=cols[:, 32:32 + LT4], func=AF.Ln,
            scale=1.0 / D, bias=EPS))
        k.op("scalar", lambda e, cols=cols: e.activation(
            out=cols[:, 36:36 + LT4], in_=cols[:, 36:36 + LT4], func=AF.Exp,
            scale=-0.5))
        for lt in range(LT4):
            k.op("scalar", lambda e, lt=lt, cols=cols: e.activation(
                out=xnb[:, lt * D:(lt + 1) * D], in_=xnf32[:, lt * D:(lt + 1) * D],
                func=AF.Identity, scale=cols[:, 36 + lt:37 + lt]),
                waits=([(k.sem("p4_tr"), DT * c)] if lt == 0 and c >= 1 else []),
                incs=([(s_xn, 1)] if lt == LT4 - 1 else []))
        s_tr, s_ev = emit_transpose(
            k, "p4", xnb, xt, ps_tr, c, LT4,
            [(s_xn, c + 1)] + ([(s_bias, biasN)] if c == 0 else []))
        if globals().get("P4_STAGE", 99) < 2:
            continue
        for dt2 in range(DT):
            j = c * DT + dt2
            for dt in range(DT):
                k.op("tensor", lambda e, dt=dt, dt2=dt2, xt=xt: e.matmul(
                    ps_k[:, 0:CH4], wk[:, dt * D + dt2 * 128:dt * D + (dt2 + 1) * 128],
                    xt[:, dt * CH4:(dt + 1) * CH4],
                    start=(dt == 0), stop=False),
                    waits=([(s_ev, DT * (c + 1))] if dt == 0 and dt2 == 0 else []) +
                          ([(s_gf, gfN)] if c == 0 and dt == 0 and dt2 == 0
                           else []) +
                          ([(s_kev, j)] if dt == 0 and j >= 1 else []))
            k.op("tensor", lambda e, dt2=dt2: e.matmul(
                ps_k[:, 0:CH4], biask[0:1, dt2 * 128:(dt2 + 1) * 128],
                ones_bf[0:1, :], start=False, stop=True), incs=[(s_mm_k, 1)])
            k.op("scalar", lambda e, dt2=dt2, kt=kt: e.activation(
                out=kt[:, dt2 * CH4:(dt2 + 1) * CH4], in_=ps_k[:, 0:CH4],
                func=AF.Identity),
                waits=[(s_mm_k, j + 1)] + ([(s_t1, 2 * (c - 1))] if dt2 == 0 and c >= 2 and globals().get("P4_STAGE", 99) >= 3 else []),
                incs=[(s_kev, 1)])
            k.op("vector", lambda e, dt2=dt2, kt=kt: e.tensor_copy(
                keysbf[:, dt2 * CH4:(dt2 + 1) * CH4],
                kt[:, dt2 * CH4:(dt2 + 1) * CH4]),
                waits=[(s_kev, j + 1)] +
                      ([(s_mm_ka, DT * c)] if dt2 == 0 and c >= 1 else []),
                incs=[(s_kbf, 1)])
        for dt2 in range(DT):
            j = c * DT + dt2
            for dt in range(DT):
                k.op("tensor", lambda e, dt=dt, dt2=dt2, xt=xt: e.matmul(
                    ps_v[:, 0:CH4], wv[:, dt * D + dt2 * 128:dt * D + (dt2 + 1) * 128],
                    xt[:, dt * CH4:(dt + 1) * CH4],
                    start=(dt == 0), stop=False),
                    waits=[(s_vev, j)] if dt == 0 and j >= 1 else [])
            k.op("tensor", lambda e, dt2=dt2: e.matmul(
                ps_v[:, 0:CH4], biasv[0:1, dt2 * 128:(dt2 + 1) * 128],
                ones_bf[0:1, :], start=False, stop=True), incs=[(s_mm_v, 1)])
            k.op("scalar", lambda e, dt2=dt2, vt=vt: e.activation(
                out=vt[:, dt2 * CH4:(dt2 + 1) * CH4], in_=ps_v[:, 0:CH4],
                func=AF.Identity),
                waits=[(s_mm_v, j + 1)] + ([(s_at, 2 * (c - 1))] if dt2 == 0 and c >= 2 and globals().get("P4_STAGE", 99) >= 4 else []),
                incs=[(s_vev, 1)])
        for dt2 in range(DT):
            j = c * DT + dt2
            for dt in range(DT):
                k.op("tensor", lambda e, dt=dt, dt2=dt2: e.matmul(
                    ps_ka[:, 0:CH4],
                    wka[:, dt * D + dt2 * 128:dt * D + (dt2 + 1) * 128],
                    keysbf[:, dt * CH4:(dt + 1) * CH4],
                    start=(dt == 0), stop=(dt == DT - 1)),
                    waits=([(s_kbf, DT * (c + 1))] if dt == 0 and dt2 == 0 else []) +
                          ([(s_kw, j)] if dt == 0 and j >= 1 else []),
                    incs=[(s_mm_ka, 1)] if dt == DT - 1 else [])
            k.op("scalar", lambda e, dt2=dt2: e.activation(
                out=kwT[:, dt2 * CH4:(dt2 + 1) * CH4], in_=ps_ka[:, 0:CH4],
                func=AF.Exp, scale=1.0 / 32.0, bias=bkab[:, dt2:dt2 + 1]),
                waits=[(s_mm_ka, j + 1)] +
                      (([(s_scs, DT * c), (s_t1, 2 * c)] if globals().get("P4_STAGE", 99) >= 3 else []) +
                       ([(s_otr, DT * LT4 * c)] if globals().get("P4_STAGE", 99) >= 4 else [])
                       if dt2 == 0 and c >= 1 else []),
                incs=[(s_kw, 1)])
        if globals().get("P4_STAGE", 99) < 3:
            continue
        HW4 = (DT // 2) * CH4
        for dt2 in range(DT):
            k.op("vector", lambda e, dt2=dt2: e.tensor_tensor_scan(
                sT[:, dt2 * CH4:(dt2 + 1) * CH4], ones_f[:, 0:CH4],
                kwT[:, dt2 * CH4:(dt2 + 1) * CH4],
                k.g["sstate"][:, dt2:dt2 + 1], OP.mult, OP.add),
                waits=[(s_kw, c * DT + dt2 + 1)] +
                      ([(s_lnS, 2 * (c - 1) + dt2 // 4 + 1)] if c >= 1 else []),
                incs=[(s_scs, 1)])
        k.op("vector", lambda e: e.tensor_copy(
            k.g["sstate"][:, 0:DT], sT[:, CH4 - 1::CH4]))
        for h in range(2):
            hs = slice(h * HW4, (h + 1) * HW4)
            k.op("scalar", lambda e, hs=hs: e.activation(
                out=pT[:, hs], in_=sT[:, hs], func=AF.Ln),
                waits=[(s_scs, c * DT + 4 * (h + 1))] +
                      ([(s_at, 2 * (c - 1) + h + 1)] if c >= 1 and globals().get("P4_STAGE", 99) >= 4 else []))
            k.op("scalar", lambda e, hs=hs: e.activation(
                out=ivT[:, hs], in_=pT[:, hs], func=AF.Exp, scale=-1.0),
                waits=[(s_kk, 2 * (c - 1) + h + 1)] if c >= 1 else [],
                incs=[(s_lnS, 1)])
        for h in range(2):
            hs = slice(h * HW4, (h + 1) * HW4)
            k.op("gpsimd", lambda e, kt=kt, hs=hs: e.tensor_tensor(
                out=t1T[:, hs], in0=kwT[:, hs], in1=kt[:, hs], op=OP.mult),
                waits=[(s_kw, c * DT + 4 * (h + 1)), (s_kev, c * DT + 4 * (h + 1))] +
                      ([(s_kk, 2 * (c - 1) + h + 1)] if c >= 1 else []),
                incs=[(s_t1, 1)])
        for h in range(2):
            hs = slice(h * HW4, (h + 1) * HW4)
            k.op("gpsimd", lambda e, hs=hs: e.tensor_tensor(
                out=kkT[:, hs], in0=t1T[:, hs], in1=ivT[:, hs], op=OP.mult),
                waits=[(s_t1, 2 * c + h + 1), (s_lnS, 2 * c + h + 1)] +
                      ([(s_pcs, (c - 1) * DT + 4 * (h + 1))] if c >= 1 else []),
                incs=[(s_kk, 1)])
        for dt2 in range(DT):
            k.op("vector", lambda e, dt2=dt2: e.tensor_tensor_scan(
                pT[:, dt2 * CH4:(dt2 + 1) * CH4], ones_f[:, 0:CH4],
                kkT[:, dt2 * CH4:(dt2 + 1) * CH4],
                k.g["pstate"][:, dt2:dt2 + 1], OP.mult, OP.add),
                waits=[(s_kk, 2 * c + dt2 // 4 + 1)] +
                      ([(s_at, 2 * (c - 1) + dt2 // 4 + 1)] if c >= 1 and globals().get("P4_STAGE", 99) >= 4 else []),
                incs=[(s_pcs, 1)])
        k.op("vector", lambda e: e.tensor_copy(
            k.g["pstate"][:, 0:DT], pT[:, CH4 - 1::CH4]))
        if globals().get("P4_STAGE", 99) < 4:
            continue
        for h in range(2):
            hs = slice(h * HW4, (h + 1) * HW4)
            k.op("gpsimd", lambda e, vt=vt, hs=hs: e.tensor_tensor(
                out=kwT[:, hs], in0=pT[:, hs], in1=vt[:, hs], op=OP.mult),
                waits=[(s_pcs, c * DT + 4 * (h + 1)), (s_vev, c * DT + 4 * (h + 1))] +
                      ([(s_otr, DT * LT4 * c)] if c >= 1 else []),
                incs=[(s_at, 1)])
        for lt in range(LT4):
            m = LT4 * c + lt
            for dt in range(DT):
                k.op("tensor", lambda e, lt=lt, dt=dt: e.transpose(
                    ps_ot[:, dt * 128:(dt + 1) * 128],
                    kwT[:, dt * CH4 + lt * 128:dt * CH4 + (lt + 1) * 128],
                    k.g["identf"][:]),
                    waits=([(s_at, 2 * c + dt // 4 + 1)] if dt in (0, 4) else []) +
                          ([(s_x3, m)] if dt == 0 and m >= 1 else []),
                    incs=[(s_otr, 1)])
            k.op("vector", lambda e, lt=lt, rb=rb, c=c: e.scalar_tensor_tensor(
                out=x3b[c % 2][:, lt * D:(lt + 1) * D], in0=ps_ot[:], scalar=0.0,
                in1=rb[:, lt * D:(lt + 1) * D], op0=OP.bypass, op1=OP.add),
                waits=[(s_otr, DT * (m + 1))],
                incs=[(s_x3, 1)] + ([(s_rb_v, 1)] if lt == LT4 - 1 else []))
        k.op("sync", lambda e, c=c, cs=cs: e.dma_start(
            out=hb["rs3_s"][cs, :].rearrange("(lt p) d -> p lt d", p=128),
            in_=x3b[c % 2][:].rearrange("p (lt d) -> p lt d", lt=LT4)),
            waits=[(s_x3, LT4 * (c + 1))] +
                  ([(s_x3_st, 16 * (c - 1))] if c >= 2 else []),
            incs=[(s_x3_st, 16)])
    k.op("sync", lambda e: None, waits=[(s_x3_st, s_x3_st.n)])



def build_nc(passes=_BUILD_PASSES, debug=()):
    nc = bass.Bass("TRN2", target_bir_lowering=False,
                   detect_race_conditions=False)
    dbg = set(debug)

    hb = {}
    hb["x0"] = nc.dram_tensor("x0", [L, D], F32, kind="ExternalInput").ap()
    for nm, shape, dt in (
        ("skipT", [D, E], F32), ("inT", [D, E], F32), ("outT", [E, D], F32),
        ("wd1T", [E, DDL], F32), ("wd2T", [DDL, E], F32),
        ("wbT", [E, N], F32), ("wcT", [E, N], F32),
        ("normw_col", [128, DT], F32), ("convw_col", [128, ET * 4], F32),
        ("convb_col", [128, ET], F32), ("wd2b_col", [128, ET], F32),
        ("wD_col", [128, ET], F32), ("alog_col", [128, ET * N], F32),
        ("wgT", [D, H], I32), ("wuT", [D, H], I32), ("wdT", [H, D], I32),
        ("wkT", [D, D], F32), ("wkaT", [D, D], F32), ("wvT", [D, D], F32),
        ("gamma_col", [128, DT], F32), ("beta_col", [128, DT], F32),
        ("bka_col", [128, DT], F32), ("bk_row", [1, D], F32), ("bv_row", [1, D], F32),
        ("ident", [128, 128], BF16), ("identf", [128, 128], F32),
    ):
        hb[nm] = nc.dram_tensor(nm, shape, dt, kind="ExternalInput").ap()
    hb["out"] = nc.dram_tensor("out", [L, D], F32, kind="ExternalOutput").ap()

    def scratch(nm, shape, dt):
        kind = "ExternalOutput" if nm in dbg else "Internal"
        hb[nm] = nc.dram_tensor(nm, shape, dt, kind=kind).ap()
    scratch("xpl_s", [L, D], F32)
    scratch("xc_s", [E, L], F32)
    scratch("ss_s", [E, L], F32)
    scratch("d1_s", [DDL, L], BF16)
    scratch("bm_s", [N, L], F32)
    scratch("cm_s", [N, L], F32)
    scratch("bmb_s", [N, L], BF16)
    scratch("cmb_s", [N, L], BF16)
    scratch("rs1_s", [L, D], F32)
    scratch("rs2_s", [L, D], F32)
    scratch("rs3_s", [L, D], F32)

    stack = ExitStack()
    with stack:
        k = K(nc, stack)
        ec = stack.enter_context
        k.g["ident"] = ec(nc.sbuf_tensor("identg", [128, 128], BF16))
        k.g["identf"] = ec(nc.sbuf_tensor("identfg", [128, 128], F32))
        k.g["state"] = ec(nc.sbuf_tensor("stateg", [128, ET * N], F32))
        k.g["sstate"] = ec(nc.sbuf_tensor("sstateg", [128, DT], F32))
        k.g["pstate"] = ec(nc.sbuf_tensor("pstateg", [128, DT], F32))
        k.g["sqscr"] = ec(nc.sbuf_tensor("sqscrg", [128, D], F32))

        CONSTS = [EPS, 1.0, -0.5, -1.0, 1.0 / D, -1.0 / D, 1.0 / 32.0, FFS]
        constc = ec(nc.sbuf_tensor("constc", [128, len(CONSTS)], F32))
        s0 = k.sem("init")
        for i, v in enumerate(CONSTS):
            nc.const_aps.register(F32, v, constc[:, i:i + 1])
            k.op("vector", lambda e, i=i, v=v: e.memset(constc[:, i:i + 1], v),
                 incs=[(s0, 1)])
        k.op("sync", lambda e: e.dma_start(out=k.g["ident"][:], in_=hb["ident"][:]),
             incs=[(s0, 16)])
        k.op("sync", lambda e: e.dma_start(out=k.g["identf"][:], in_=hb["identf"][:]),
             incs=[(s0, 16)])
        k.op("vector", lambda e: e.memset(k.g["state"][:], 0.0), incs=[(s0, 1)])
        k.op("vector", lambda e: e.memset(k.g["sstate"][:], 0.0), incs=[(s0, 1)])
        k.op("vector", lambda e: e.memset(k.g["pstate"][:], 0.0), incs=[(s0, 1)])
        t0 = s0.n
        for eng in ENGS:
            k.op(eng, lambda e: None, waits=[(s0, t0)])

        if 1 in passes:
            k.begin_pass()
            with ExitStack() as p:
                emit_p1(k, p.enter_context, hb)
            k.end_pass()
        if 2 in passes:
            k.begin_pass()
            with ExitStack() as p:
                emit_p2(k, p.enter_context, hb)
            k.end_pass()
        if 3 in passes:
            k.begin_pass()
            with ExitStack() as p:
                emit_ffn(k, p.enter_context, hb, "f1", hb["rs1_s"], hb["rs2_s"])
            k.end_pass()
        if 4 in passes:
            k.begin_pass()
            with ExitStack() as p:
                emit_p4(k, p.enter_context, hb)
            k.end_pass()
        if 5 in passes:
            k.begin_pass()
            with ExitStack() as p:
                emit_ffn(k, p.enter_context, hb, "f2", hb["rs3_s"], hb["out"])
            k.end_pass()

        with nc.Block() as block:
            @block.sync
            def _(e):
                for fn in k.plan["sync"]:
                    fn(e)

            @block.vector
            def _(e):
                for fn in k.plan["vector"]:
                    fn(e)

            @block.scalar
            def _(e):
                for fn in k.plan["scalar"]:
                    fn(e)

            @block.tensor
            def _(e):
                for fn in k.plan["tensor"]:
                    fn(e)

            @block.gpsimd
            def _(e):
                for fn in k.plan["gpsimd"]:
                    fn(e)
    return nc



def _prep_inputs(inputs):
    t = lambda a: np.ascontiguousarray(np.asarray(a).T)
    col = lambda v, kk: np.ascontiguousarray(np.asarray(v).reshape(kk, 128).T)
    g = {}
    g["skipT"] = t(inputs["L0_attn_a_skip_w"])
    g["inT"] = t(inputs["L0_attn_a_in_w"])
    g["outT"] = t(inputs["L0_attn_a_out_w"])
    g["wd1T"] = t(inputs["L0_attn_a_Wd1"])
    g["wd2T"] = t(inputs["L0_attn_a_Wd2_w"])
    g["wbT"] = t(inputs["L0_attn_a_WB"])
    g["wcT"] = t(inputs["L0_attn_a_WC"])
    g["normw_col"] = col(inputs["L0_attn_a_norm_w"], DT)
    g["convb_col"] = col(inputs["L0_attn_a_conv_b"], ET)
    g["wd2b_col"] = col(inputs["L0_attn_a_Wd2_b"], ET)
    g["wD_col"] = col(inputs["L0_attn_a_W_D"], ET)
    cw = np.asarray(inputs["L0_attn_a_conv_w"])[:, 0, :]
    g["convw_col"] = np.ascontiguousarray(
        cw.reshape(ET, 128, 4).transpose(1, 0, 2).reshape(128, ET * 4))
    al = np.asarray(inputs["L0_attn_a_A_log"])
    g["alog_col"] = np.ascontiguousarray(
        al.reshape(ET, 128, N).transpose(1, 0, 2).reshape(128, ET * N))
    g["wgT"] = t(inputs["L0_ffn_w_gate"])
    g["wuT"] = t(inputs["L0_ffn_w_up"])
    g["wdT"] = t(inputs["L0_ffn_w_down"])
    g["wkT"] = t(inputs["L0_attn_b_Wk"])
    g["wkaT"] = t(inputs["L0_attn_b_Wka"])
    g["wvT"] = t(inputs["L0_attn_b_Wv"])
    g["gamma_col"] = col(inputs["L0_attn_b_gamma"], DT)
    g["beta_col"] = col(inputs["L0_attn_b_beta"], DT)
    g["bka_col"] = col(inputs["L0_attn_b_bka"], DT)
    g["bk_row"] = np.ascontiguousarray(
        np.asarray(inputs["L0_attn_b_bk"]).reshape(1, D))
    g["bv_row"] = np.ascontiguousarray(
        np.asarray(inputs["L0_attn_b_bv"]).reshape(1, D))
    g["ident"] = np.eye(128, dtype=np.float32).astype(ml_dtypes.bfloat16)
    g["identf"] = np.eye(128, dtype=np.float32)
    return g


def kernel(**inputs):
    resid = np.asarray(inputs["L0_attn_a_resid"])
    g = _prep_inputs(inputs)
    nc = build_nc()
    in_maps = []
    for b in range(8):
        m = dict(g)
        m["x0"] = np.ascontiguousarray(resid[b])
        in_maps.append(m)
    res = run_bass_kernel_spmd(nc, in_maps, core_ids=list(range(8)))
    return np.stack([np.asarray(res.results[b]["out"]) for b in range(8)], axis=0)


# version 13 (final)
# speedup vs baseline: 1.0263x; 1.0263x over previous
import numpy as np
import ml_dtypes
from contextlib import ExitStack

import concourse.bass as bass
import concourse.mybir as mybir
from concourse.bass_utils import run_bass_kernel_spmd
from concourse.alu_op_type import AluOpType as OP

F32 = mybir.dt.float32
BF16 = mybir.dt.bfloat16
I32 = mybir.dt.int32
AF = mybir.ActivationFunctionType
AX = mybir.AxisListType

L = 4096
D = 1024
E = 2048
N = 16
DDL = 64
H = 3072
CH = 512
CH4 = 256
CHF = 256
ET = E // 128
DT = D // 128
HT = H // 128
EPS = 1e-5
FFS = 1.0 / 127.0

ENGS = ("sync", "vector", "scalar", "tensor", "gpsimd")

_BUILD_PASSES = (1, 2, 3, 4, 5)


class Sem:
    __slots__ = ("h", "n")

    def __init__(self, h):
        self.h = h
        self.n = 0


NSLOTS = 32


class Slot:
    __slots__ = ("h", "n")

    def __init__(self, h):
        self.h = h
        self.n = 0


class SemRef:
    __slots__ = ("slot", "base")

    def __init__(self, slot, base):
        self.slot = slot
        self.base = base

    @property
    def n(self):
        return self.slot.n - self.base


class K:
    def __init__(self, nc, stack):
        self.nc = nc
        self.stack = stack
        self.plan = {e: [] for e in ENGS}
        self.sems = {}
        self.g = {}
        self.slots = [Slot(stack.enter_context(nc.semaphore(f"slot{i}")))
                      for i in range(NSLOTS)]
        self.scope = None
        self.scope_used = 0

    def sem(self, name):
        if self.scope is not None and not name.startswith(("init", "bar")):
            if name not in self.scope:
                assert self.scope_used < NSLOTS, "out of sem slots"
                slot = self.slots[self.scope_used]
                self.scope[name] = SemRef(slot, slot.n)
                self.scope_used += 1
            return self.scope[name]
        if name not in self.sems:
            slot = Slot(self.stack.enter_context(self.nc.semaphore(name)))
            self.sems[name] = SemRef(slot, 0)
        return self.sems[name]

    def op(self, eng, build, waits=(), incs=()):
        w = [(s.slot.h, s.base + v) for (s, v) in waits if v > 0]
        pl = []
        for (s, amt) in incs:
            if amt <= 0:
                continue
            s.slot.n += amt
            pl.append((s.slot.h, amt))

        def fn(e):
            for (hh, v) in w:
                e.wait_ge(hh, v)
            ins = build(e)
            drained = False
            for idx, (hh, amt) in enumerate(pl):
                if ins is not None and idx == 0:
                    ins.then_inc(hh, amt)
                else:
                    if not drained and eng != "sync":
                        e.drain()
                        drained = True
                    e.sem_inc(hh, amt)
        self.plan[eng].append(fn)

    def barrier(self):
        b = self.sem("bar")
        for eng in ENGS:
            def bld(e, eng=eng):
                if eng != "sync":
                    e.drain()
                return None
            self.op(eng, bld, incs=[(b, 1)])
        t = b.n
        for eng in ENGS:
            self.op(eng, lambda e: None, waits=[(b, t)])

    def fresh(self, name):
        if name not in self.sems:
            slot = Slot(self.stack.enter_context(self.nc.semaphore(name)))
            self.sems[name] = SemRef(slot, 0)
        return self.sems[name]

    def begin_pass(self):
        self.scope = {}
        self.scope_used = 0

    def end_pass(self):
        self.scope = None
        self.barrier()


def emit_ln_stats(k, pre, rbuf, cols, sqscr, c, nlt, waits_first):
    s_rs = k.sem(pre + "_rs")
    s_rb_a = k.sem(pre + "_rb_a")
    for lt in range(nlt):
        k.op("scalar", lambda e, lt=lt: e.activation(
            out=sqscr[:], in_=rbuf[:, lt * D:(lt + 1) * D], func=AF.Identity,
            accum_out=cols[:, lt:lt + 1]),
            waits=list(waits_first) if lt == 0 else [])
    for lt in range(nlt):
        k.op("scalar", lambda e, lt=lt: e.activation(
            out=sqscr[:], in_=rbuf[:, lt * D:(lt + 1) * D], func=AF.Square,
            accum_out=cols[:, 4 + lt:5 + lt]))
    k.op("scalar", lambda e: e.activation(
        out=cols[:, 8:8 + nlt], in_=cols[:, 0:nlt], func=AF.Identity,
        scale=-1.0 / D))
    k.op("scalar", lambda e: e.activation(
        out=cols[:, 16:16 + nlt], in_=cols[:, 8:8 + nlt], func=AF.Square))
    k.op("scalar", lambda e: e.activation(
        out=cols[:, 20:20 + nlt], in_=cols[:, 16:16 + nlt], func=AF.Identity,
        scale=-1.0))
    k.op("scalar", lambda e: e.activation(
        out=cols[:, 24:24 + nlt], in_=cols[:, 4:4 + nlt], func=AF.Identity,
        scale=1.0 / D))
    for lt in range(nlt):
        k.op("scalar", lambda e, lt=lt: e.activation(
            out=cols[:, 28 + lt:29 + lt], in_=cols[:, 24 + lt:25 + lt],
            func=AF.Identity, bias=cols[:, 20 + lt:21 + lt]))
    k.op("scalar", lambda e: e.activation(
        out=cols[:, 28:28 + nlt], in_=cols[:, 28:28 + nlt], func=AF.Ln, bias=EPS))
    k.op("scalar", lambda e: e.activation(
        out=cols[:, 12:12 + nlt], in_=cols[:, 28:28 + nlt], func=AF.Exp, scale=-0.5),
        incs=[(s_rs, 1)])
    return s_rs, s_rb_a


def emit_transpose(k, pre, xnb, xt, ps_tr, c, nlt, waits_first):
    s_tr = k.sem(pre + "_tr")
    s_ev = k.sem(pre + "_ev")
    CL = nlt * 128
    for dt in range(DT):
        q = dt % 2
        g = DT * c + dt
        for lt in range(nlt):
            k.op("tensor", lambda e, dt=dt, lt=lt, q=q: e.transpose(
                ps_tr[:, q * 1024 + lt * 128:q * 1024 + (lt + 1) * 128],
                xnb[:, lt * D + dt * 128:lt * D + (dt + 1) * 128], k.g["ident"][:]),
                waits=(list(waits_first) if dt == 0 and lt == 0 else []) +
                      ([(s_ev, g - 1)] if g >= 2 and lt == 0 else []),
                incs=[(s_tr, 1)] if lt == nlt - 1 else [])
        k.op("vector", lambda e, dt=dt, q=q, xt=xt: e.tensor_copy(
            xt[:, dt * CL:(dt + 1) * CL], ps_tr[:, q * 1024:q * 1024 + CL]),
            waits=[(s_tr, g + 1)], incs=[(s_ev, 1)])
    return s_tr, s_ev



def emit_p1(k, ec, hb):
    nc = k.nc
    LT = CH // 128
    skipT = ec(nc.sbuf_tensor("p1_skipT", [128, DT * E], BF16))
    inT = ec(nc.sbuf_tensor("p1_inT", [128, DT * E], BF16))
    wd1T = ec(nc.sbuf_tensor("p1_wd1T", [128, ET * DDL], BF16))
    wbT = ec(nc.sbuf_tensor("p1_wbT", [128, ET * N], BF16))
    wcT = ec(nc.sbuf_tensor("p1_wcT", [128, ET * N], BF16))
    normw = ec(nc.sbuf_tensor("p1_normw", [128, DT], F32))
    convw = ec(nc.sbuf_tensor("p1_convw", [128, ET * 4], F32))
    convb = ec(nc.sbuf_tensor("p1_convb", [128, ET], F32))
    rbuf = [ec(nc.sbuf_tensor(f"p1_rb{i}", [128, LT * D], F32)) for i in range(2)]
    ln1 = ec(nc.sbuf_tensor("p1_ln1", [128, LT * D], F32))
    xnb = ec(nc.sbuf_tensor("p1_xnb", [128, LT * D], BF16))
    xnT = [ec(nc.sbuf_tensor("p1_xnT0", [128, DT * CH], BF16))]
    ext = ec(nc.sbuf_tensor("p1_ext", [128, ET * (CH + 3)], F32))
    cacc = [ec(nc.sbuf_tensor(f"p1_cacc{i}", [128, CH], F32)) for i in range(2)]
    ssb = [ec(nc.sbuf_tensor(f"p1_ssb{i}", [128, CH], F32)) for i in range(3)]
    xcb = [ec(nc.sbuf_tensor(f"p1_xcb{i}", [128, CH], F32)) for i in range(3)]
    xcbf = [ec(nc.sbuf_tensor(f"p1_xcbf{i}", [128, CH], BF16)) for i in range(3)]
    d1bf = ec(nc.sbuf_tensor("p1_d1bf", [DDL, CH], BF16))
    bmb = ec(nc.sbuf_tensor("p1_bmb", [N, CH], F32))
    cmb = ec(nc.sbuf_tensor("p1_cmb", [N, CH], F32))
    colsA = [ec(nc.sbuf_tensor(f"p1_cols{i}", [128, 48], F32)) for i in range(2)]

    ps_sk = [ec(nc.psum_tensor(f"p1_ps_sk{i}", [128, CH], F32)) for i in range(2)]
    ps_in1 = ec(nc.psum_tensor("p1_ps_in", [128, CH], F32))
    ps_tr = ec(nc.psum_tensor("p1_ps_tr", [128, 2048], BF16))
    ps_d1 = ec(nc.psum_tensor("p1_ps_d1", [DDL, CH], F32))
    ps_b = ec(nc.psum_tensor("p1_ps_b", [N, CH], F32))
    ps_c = ec(nc.psum_tensor("p1_ps_c", [N, CH], F32))

    s_w = k.fresh("p1_w")
    s_wv = k.sem("p1_wv")
    s_wf = k.sem("p1_wf")
    s_res = k.sem("p1_res")
    s_rb_v = k.sem("p1_rb_v")
    s_ln1 = k.sem("p1_ln1")
    s_sq2 = k.sem("p1_sq2")
    s_rs2 = k.sem("p1_rs2")
    s_xpl = k.sem("p1_xpl")
    s_xpl_st = k.sem("p1_xpl_st")
    s_xn = k.sem("p1_xn")
    s_mm_sk = k.sem("p1_mm_sk")
    s_mm_in = k.sem("p1_mm_in")
    s_ss_act = k.sem("p1_ss_act")
    s_ss_st = k.sem("p1_ss_st")
    s_evin = k.sem("p1_evin")
    s_conv = k.sem("p1_conv")
    s_xc_act = k.sem("p1_xc_act")
    s_xc_st = k.sem("p1_xc_st")
    s_xcbf = k.sem("p1_xcbf")
    s_dbc_i = k.sem("p1_dbc_i")
    s_dbc_ev = k.sem("p1_dbc_ev")
    s_dbc_st = k.sem("p1_dbc_st")

    for dt in range(DT):
        k.op("gpsimd", lambda e, dt=dt: e.dma_start(
            out=skipT[:, dt * E:(dt + 1) * E],
            in_=hb["skipT"][dt * 128:(dt + 1) * 128, :]), incs=[(s_w, 16)])
        k.op("gpsimd", lambda e, dt=dt: e.dma_start(
            out=inT[:, dt * E:(dt + 1) * E],
            in_=hb["inT"][dt * 128:(dt + 1) * 128, :]), incs=[(s_w, 16)])
    for et in range(ET):
        k.op("gpsimd", lambda e, et=et: e.dma_start(
            out=wd1T[:, et * DDL:(et + 1) * DDL],
            in_=hb["wd1T"][et * 128:(et + 1) * 128, :]), incs=[(s_w, 16)])
        k.op("gpsimd", lambda e, et=et: e.dma_start(
            out=wbT[:, et * N:(et + 1) * N],
            in_=hb["wbT"][et * 128:(et + 1) * 128, :]), incs=[(s_w, 16)])
        k.op("gpsimd", lambda e, et=et: e.dma_start(
            out=wcT[:, et * N:(et + 1) * N],
            in_=hb["wcT"][et * 128:(et + 1) * 128, :]), incs=[(s_w, 16)])
    for nm, t in (("normw_col", normw), ("convw_col", convw), ("convb_col", convb)):
        k.op("sync", lambda e, nm=nm, t=t: e.dma_start(out=t[:], in_=hb[nm][:]),
             incs=[(s_wv, 16)])
    wN, wvN = s_w.n, s_wv.n

    wstage = ec(nc.sbuf_tensor("p1_wstage", [128, E], BF16))
    for dt in range(DT):
        k.op("vector", lambda e, dt=dt: e.tensor_copy(
            wstage[:], skipT[:, dt * E:(dt + 1) * E]),
            waits=[(s_w, wN), (s_wv, wvN)] if dt == 0 else [])
        k.op("vector", lambda e, dt=dt: e.tensor_scalar(
            out=skipT[:, dt * E:(dt + 1) * E], in0=wstage[:],
            scalar1=normw[:, dt:dt + 1], scalar2=None, op0=OP.mult))
        k.op("vector", lambda e, dt=dt: e.tensor_copy(
            wstage[:], inT[:, dt * E:(dt + 1) * E]))
        k.op("vector", lambda e, dt=dt: e.tensor_scalar(
            out=inT[:, dt * E:(dt + 1) * E], in0=wstage[:],
            scalar1=normw[:, dt:dt + 1], scalar2=None, op0=OP.mult))
    for et in range(ET):
        k.op("vector", lambda e, et=et: e.memset(
            ext[:, et * (CH + 3):et * (CH + 3) + 3], 0.0),
            incs=[(s_wf, 1)] if et == ET - 1 else [])
    wfN = s_wf.n

    NCH = L // CH
    for c in range(NCH):
        cs = slice(c * CH, (c + 1) * CH)
        rb = rbuf[c % 2]
        xt = xnT[0]
        cols = colsA[c % 2]

        k.op("sync", lambda e, rb=rb, cs=cs: e.dma_start(
            out=rb[:].rearrange("p (lt d) -> p lt d", lt=LT),
            in_=hb["x0"][cs, :].rearrange("(lt p) d -> p lt d", p=128)),
            waits=([(k.sem("p1_rb_a"), c - 1)] if c >= 2 else []),
            incs=[(s_res, 16)])

        wf = [(s_res, 16 * (c + 1))] + ([(s_wf, wfN)] if c == 0 else [])
        s_rs, s_rb_a = emit_ln_stats(k, "p1", rb, cols, k.g["sqscr"], c, LT, wf)
        for lt in range(LT):
            k.op("scalar", lambda e, rb=rb, lt=lt, cols=cols: e.activation(
                out=ln1[:, lt * D:(lt + 1) * D], in_=rb[:, lt * D:(lt + 1) * D],
                func=AF.Identity, bias=cols[:, 8 + lt:9 + lt]),
                waits=([(s_sq2, c)] if lt == 0 and c >= 1 else []) +
                      ([(s_xpl_st, 16 * c)] if lt == 0 and c >= 1 else []))
            k.op("scalar", lambda e, lt=lt, cols=cols: e.activation(
                out=ln1[:, lt * D:(lt + 1) * D], in_=ln1[:, lt * D:(lt + 1) * D],
                func=AF.Identity, scale=cols[:, 12 + lt:13 + lt]),
                incs=([(s_ln1, 1), (s_xpl, 1), (s_rb_a, 1)] if lt == LT - 1 else []))
        k.op("sync", lambda e, cs=cs: e.dma_start(
            out=hb["xpl_s"][cs, :].rearrange("(lt p) d -> p lt d", p=128),
            in_=ln1[:].rearrange("p (lt d) -> p lt d", lt=LT)),
            waits=[(s_xpl, c + 1)], incs=[(s_xpl_st, 16)])
        for lt in range(LT):
            k.op("scalar", lambda e, lt=lt, cols=cols: e.activation(
                out=k.g["sqscr"][:], in_=ln1[:, lt * D:(lt + 1) * D], func=AF.Square,
                accum_out=cols[:, 32 + lt:33 + lt]))
        k.op("scalar", lambda e, cols=cols: e.activation(
            out=cols[:, 36:36 + LT], in_=cols[:, 32:32 + LT], func=AF.Ln,
            scale=1.0 / D, bias=EPS))
        k.op("scalar", lambda e, cols=cols: e.activation(
            out=cols[:, 36:36 + LT], in_=cols[:, 36:36 + LT], func=AF.Exp,
            scale=-0.5),
            incs=[(s_sq2, 1)])
        for lt in range(LT):
            k.op("scalar", lambda e, lt=lt, cols=cols: e.activation(
                out=xnb[:, lt * D:(lt + 1) * D], in_=ln1[:, lt * D:(lt + 1) * D],
                func=AF.Identity, scale=cols[:, 36 + lt:37 + lt]),
                waits=([(k.sem("p1_tr"), DT * c)] if lt == 0 and c >= 1 else []),
                incs=[(s_xn, 1)] if lt == LT - 1 else [])
        s_tr, s_ev = emit_transpose(k, "p1", xnb, xt, ps_tr, c, LT,
                                    [(s_xn, c + 1)])

        for et in range(ET):
            j = c * ET + et
            pk, pi = ps_sk[et % 2], ps_in1
            for dt in range(DT):
                k.op("tensor", lambda e, dt=dt, et=et, pk=pk, xt=xt: e.matmul(
                    pk[:], skipT[:, dt * E + et * 128:dt * E + (et + 1) * 128],
                    xt[:, dt * CH:(dt + 1) * CH],
                    start=(dt == 0), stop=(dt == DT - 1)),
                    waits=([(s_ev, DT * (c + 1))] if dt == 0 and et == 0 else []) +
                          ([(s_ss_act, j - 1)] if dt == 0 and j >= 2 else []),
                    incs=[(s_mm_sk, 1)] if dt == DT - 1 else [])
            for dt in range(DT):
                k.op("tensor", lambda e, dt=dt, et=et, pi=pi, xt=xt: e.matmul(
                    pi[:], inT[:, dt * E + et * 128:dt * E + (et + 1) * 128],
                    xt[:, dt * CH:(dt + 1) * CH],
                    start=(dt == 0), stop=(dt == DT - 1)),
                    waits=[(s_evin, j)] if dt == 0 and j >= 1 else [],
                    incs=[(s_mm_in, 1)] if dt == DT - 1 else [])
            sst = ssb[j % 3]
            k.op("scalar", lambda e, pk=pk, sst=sst: e.activation(
                out=sst[:], in_=pk[:], func=AF.Silu),
                waits=[(s_mm_sk, j + 1)] + ([(s_ss_st, 16 * (j - 2))] if j >= 3 else []),
                incs=[(s_ss_act, 1)])
            k.op("sync", lambda e, sst=sst, et=et, cs=cs: e.dma_start(
                out=hb["ss_s"][et * 128:(et + 1) * 128, cs], in_=sst[:]),
                waits=[(s_ss_act, j + 1)], incs=[(s_ss_st, 16)])
            eb = ext[:, et * (CH + 3):(et + 1) * (CH + 3)]
            k.op("vector", lambda e, pi=pi, eb=eb: e.tensor_copy(eb[:, 3:CH + 3], pi[:]),
                 waits=[(s_mm_in, j + 1)], incs=[(s_evin, 1)])
            ca, cb2 = cacc
            k.op("vector", lambda e, eb=eb, et=et, ca=ca: e.tensor_scalar(
                out=ca[:], in0=eb[:, 0:CH], scalar1=convw[:, et * 4:et * 4 + 1],
                scalar2=None, op0=OP.mult))
            k.op("vector", lambda e, eb=eb, et=et, ca=ca, cb2=cb2: e.scalar_tensor_tensor(
                out=cb2[:], in0=eb[:, 1:CH + 1], scalar=convw[:, et * 4 + 1:et * 4 + 2],
                in1=ca[:], op0=OP.mult, op1=OP.add))
            k.op("vector", lambda e, eb=eb, et=et, ca=ca, cb2=cb2: e.scalar_tensor_tensor(
                out=ca[:], in0=eb[:, 2:CH + 2], scalar=convw[:, et * 4 + 2:et * 4 + 3],
                in1=cb2[:], op0=OP.mult, op1=OP.add))
            k.op("vector", lambda e, eb=eb, et=et, ca=ca, cb2=cb2: e.scalar_tensor_tensor(
                out=cb2[:], in0=eb[:, 3:CH + 3], scalar=convw[:, et * 4 + 3:et * 4 + 4],
                in1=ca[:], op0=OP.mult, op1=OP.add))
            k.op("vector", lambda e, eb=eb: e.tensor_copy(eb[:, 0:3], eb[:, CH:CH + 3]),
                 incs=[(s_conv, 1)])
            xct = xcb[j % 3]
            k.op("scalar", lambda e, cb2=cb2, et=et, xct=xct: e.activation(
                out=xct[:], in_=cb2[:], func=AF.Silu, bias=convb[:, et:et + 1]),
                waits=[(s_conv, j + 1)] + ([(s_xc_st, 16 * (j - 2))] if j >= 3 else []),
                incs=[(s_xc_act, 1)])
            k.op("sync", lambda e, xct=xct, et=et, cs=cs: e.dma_start(
                out=hb["xc_s"][et * 128:(et + 1) * 128, cs], in_=xct[:]),
                waits=[(s_xc_act, j + 1)], incs=[(s_xc_st, 16)])
            xbf = xcbf[j % 3]
            k.op("vector", lambda e, xct=xct, xbf=xbf: e.tensor_copy(xbf[:], xct[:]),
                 waits=[(s_xc_act, j + 1)] + ([(s_dbc_i, j - 2)] if j >= 3 else []),
                 incs=[(s_xcbf, 1)])
            for et2 in ([et - 2] if et >= 2 else []) + \
                       ([ET - 2, ET - 1] if et == ET - 1 else []):
                j2 = c * ET + et2
                xb2 = xcbf[j2 % 3]
                k.op("tensor", lambda e, et2=et2, xb2=xb2: e.matmul(
                    ps_d1[:], wd1T[:, et2 * DDL:(et2 + 1) * DDL], xb2[:],
                    start=(et2 == 0), stop=(et2 == ET - 1)),
                    waits=[(s_xcbf, j2 + 1)] +
                          ([(s_dbc_ev, 3 * c)] if et2 == 0 and c >= 1 else []))
                k.op("tensor", lambda e, et2=et2, xb2=xb2: e.matmul(
                    ps_b[:], wbT[:, et2 * N:(et2 + 1) * N], xb2[:],
                    start=(et2 == 0), stop=(et2 == ET - 1)))
                k.op("tensor", lambda e, et2=et2, xb2=xb2: e.matmul(
                    ps_c[:], wcT[:, et2 * N:(et2 + 1) * N], xb2[:],
                    start=(et2 == 0), stop=(et2 == ET - 1)),
                    incs=[(s_dbc_i, 1)])
        s_bcb = k.fresh("p1_bcb")
        k.op("vector", lambda e: e.tensor_copy(d1bf[:], ps_d1[:]),
             waits=[(s_dbc_i, ET * (c + 1))] +
                   ([(s_dbc_st, 48 * c), (s_bcb, 32 * c)] if c >= 1 else []),
             incs=[(s_dbc_ev, 1)])
        k.op("vector", lambda e: e.tensor_copy(bmb[:], ps_b[:]), incs=[(s_dbc_ev, 1)])
        k.op("vector", lambda e: e.tensor_copy(cmb[:], ps_c[:]), incs=[(s_dbc_ev, 1)])
        k.op("sync", lambda e, cs=cs: e.dma_start(out=hb["d1_s"][:, cs], in_=d1bf[:]),
             waits=[(s_dbc_ev, 3 * c + 3)], incs=[(s_dbc_st, 16)])
        k.op("sync", lambda e, cs=cs: e.dma_start(out=hb["bm_s"][:, cs], in_=bmb[:]),
             incs=[(s_dbc_st, 16)])
        k.op("sync", lambda e, cs=cs: e.dma_start(out=hb["cm_s"][:, cs], in_=cmb[:]),
             incs=[(s_dbc_st, 16)])
        k.op("gpsimd", lambda e, cs=cs: e.dma_start(out=hb["bmb_s"][:, cs], in_=bmb[:]),
             waits=[(s_dbc_ev, 3 * c + 3)], incs=[(s_bcb, 16)])
        k.op("gpsimd", lambda e, cs=cs: e.dma_start(out=hb["cmb_s"][:, cs], in_=cmb[:]),
             incs=[(s_bcb, 16)])

    k.op("sync", lambda e: None, waits=[
        (s_xpl_st, s_xpl_st.n), (s_ss_st, s_ss_st.n), (s_xc_st, s_xc_st.n),
        (s_dbc_st, s_dbc_st.n)])
    k.op("gpsimd", lambda e: None, waits=[(k.fresh("p1_bcb"), k.fresh("p1_bcb").n)])



def emit_p2(k, ec, hb):
    nc = k.nc
    LT = CH // 128
    NSP = L // CH
    outT = ec(nc.sbuf_tensor("p2_outT", [128, ET * D], BF16))
    wd2T = ec(nc.sbuf_tensor("p2_wd2T", [DDL, E], BF16))
    wd2b = ec(nc.sbuf_tensor("p2_wd2b", [128, ET], F32))
    wdd = ec(nc.sbuf_tensor("p2_wD", [128, ET], F32))
    negA = ec(nc.sbuf_tensor("p2_negA", [128, 2 * ET * N], F32))
    brep = ec(nc.sbuf_tensor("p2_brep", [128, N * CH], BF16))
    crep = ec(nc.sbuf_tensor("p2_crep", [128, N * CH], BF16))
    d1b = [ec(nc.sbuf_tensor(f"p2_d1b{i}", [DDL, CH], BF16)) for i in range(2)]
    xplb = [ec(nc.sbuf_tensor("p2_xplb0", [128, LT * D], F32))]
    xcld = [ec(nc.sbuf_tensor(f"p2_xc{i}", [128, CH], F32)) for i in range(3)]
    ssld = [ec(nc.sbuf_tensor(f"p2_ss{i}", [128, CH], F32)) for i in range(3)]
    rr = [ec(nc.sbuf_tensor(f"p2_rr{i}", [128, CH], F32)) for i in range(2)]
    aa = [ec(nc.sbuf_tensor("p2_aa0", [128, CH], F32))]
    qq = [ec(nc.sbuf_tensor("p2_qq0", [128, CH], F32))]
    pp = [ec(nc.sbuf_tensor(f"p2_pp{i}", [128, CH], F32)) for i in range(2)]
    dtl = [ec(nc.sbuf_tensor(f"p2_dt{i}", [128, CH], F32)) for i in range(2)]
    dxl = [ec(nc.sbuf_tensor(f"p2_dx{i}", [128, CH], F32)) for i in range(2)]
    abar = [ec(nc.sbuf_tensor(f"p2_ab{i}", [128, CH], F32)) for i in range(4)]
    ubuf = [ec(nc.sbuf_tensor(f"p2_ub{i}", [128, CH], F32)) for i in range(4)]
    hsl = ec(nc.sbuf_tensor("p2_hsl", [128, N * CH], F32))
    hcbuf = [ec(nc.sbuf_tensor(f"p2_hc{i}", [128, CH], BF16)) for i in range(4)]
    yf1 = [ec(nc.sbuf_tensor(f"p2_yf1{i}", [128, CH], F32)) for i in range(2)]
    yfT = ec(nc.sbuf_tensor("p2_yfT", [128, ET * CH], BF16))
    x1b = [ec(nc.sbuf_tensor(f"p2_x1b{i}", [128, D], F32)) for i in range(2)]

    ps_d2 = [ec(nc.psum_tensor(f"p2_ps_d2{i}", [128, CH], F32)) for i in range(2)]
    ps_y = [ec(nc.psum_tensor(f"p2_ps_y{i}", [128, CH], F32)) for i in range(2)]
    ps_o = [ec(nc.psum_tensor(f"p2_ps_o{i}", [128, D], F32)) for i in range(2)]

    s_w = k.fresh("p2_w")
    s_wv = k.sem("p2_wv")
    s_negA = k.sem("p2_negA")
    s_br = k.fresh("p2_br")
    s_cr = k.fresh("p2_cr")
    s_d1l = k.sem("p2_d1l")
    s_xpll = k.sem("p2_xpll")
    s_xza = k.fresh("p2_xza")
    s_xcl = k.sem("p2_xcl")
    s_ssl = k.sem("p2_ssl")
    s_mm_d2 = k.sem("p2_mm_d2")
    s_sp_rd = k.sem("p2_sp_rd")
    s_sp = k.sem("p2_sp")
    s_del = k.sem("p2_del")
    s_dx = k.sem("p2_dx")
    s_abar = k.sem("p2_abar")
    s_u = k.sem("p2_u")
    s_scan = k.sem("p2_scan")
    s_hc = k.sem("p2_hc")
    s_hcp = k.sem("p2_hcp")
    s_ymm = k.sem("p2_ymm")
    s_yf1 = k.sem("p2_yf1")
    s_yf2 = k.sem("p2_yf2")
    s_mm_out = k.sem("p2_mm_out")
    s_x1 = k.sem("p2_x1")
    s_x1_st = k.sem("p2_x1_st")

    for et in range(ET):
        k.op("gpsimd", lambda e, et=et: e.dma_start(
            out=outT[:, et * D:(et + 1) * D],
            in_=hb["outT"][et * 128:(et + 1) * 128, :]), incs=[(s_w, 16)])
    k.op("gpsimd", lambda e: e.dma_start(out=wd2T[:], in_=hb["wd2T"][:]),
         incs=[(s_w, 16)])
    for nm, t in (("wd2b_col", wd2b), ("wD_col", wdd)):
        k.op("sync", lambda e, nm=nm, t=t: e.dma_start(out=t[:], in_=hb[nm][:]),
             incs=[(s_wv, 16)])
    k.op("sync", lambda e: e.dma_start(out=negA[:, ET * N:], in_=hb["alog_col"][:]),
         incs=[(s_wv, 16)])
    wN, wvN = s_w.n, s_wv.n
    k.op("scalar", lambda e: e.activation(
        out=negA[:, ET * N:], in_=negA[:, ET * N:], func=AF.Exp),
        waits=[(s_wv, wvN)], incs=[(s_negA, 1)])
    k.op("scalar", lambda e: e.activation(
        out=negA[:, 0:ET * N], in_=negA[:, ET * N:], func=AF.Identity, scale=-1.0),
        incs=[(s_negA, 1)])

    def emit_bc_batch(sp):
        cs2 = slice(sp * CH, (sp + 1) * CH)
        for n in range(N):
            wc = []
            if sp >= 1:
                i_last = ((sp - 1) * ET + ET - 1) * N + n
                wc = [(s_hc, i_last + 1)]
            k.op("gpsimd", lambda e, n=n, cs2=cs2: e.dma_start(
                out=brep[:, n * CH:(n + 1) * CH].rearrange("p (o l) -> p o l", o=1),
                in_=hb["bmb_s"][n:n + 1, cs2].partition_broadcast(128)),
                incs=[(s_br, 16)])
            k.op("scalar", lambda e, n=n, cs2=cs2: e.dma_start(
                out=crep[:, n * CH:(n + 1) * CH].rearrange("p (o l) -> p o l", o=1),
                in_=hb["cmb_s"][n:n + 1, cs2].partition_broadcast(128)),
                waits=wc, incs=[(s_cr, 16)])

    emit_bc_batch(0)
    for s in range(NSP):
        cs = slice(s * CH, (s + 1) * CH)
        if s >= 1:
            ps_ = s - 1
            for lt in (range(LT) if globals().get("P2_STAGE", 99) >= 3 else []):
                m = LT * ps_ + lt
                k.op("sync", lambda e, ps_=ps_, lt=lt, m=m: e.dma_start(
                    out=hb["rs1_s"][ps_ * CH + lt * 128:ps_ * CH + (lt + 1) * 128, :],
                    in_=x1b[m % 2][:]),
                    waits=[(s_x1, m + 1)], incs=[(s_x1_st, 16)])

        k.op("sync", lambda e, cs=cs, s=s: e.dma_start(
            out=d1b[s % 2][:], in_=hb["d1_s"][:, cs]),
            waits=[(s_mm_d2, ET * (s - 1))] if s >= 2 else [], incs=[(s_d1l, 16)])
        k.op("sync", lambda e, cs=cs, s=s: e.dma_start(
            out=xplb[0][:].rearrange("p (lt d) -> p lt d", lt=LT),
            in_=hb["xpl_s"][cs, :].rearrange("(lt p) d -> p lt d", p=128)),
            waits=[(s_x1, LT * s)] if s >= 1 and globals().get("P2_STAGE", 99) >= 3 else [],
            incs=[(s_xpll, 16)])
        k.op("gpsimd", lambda e, cs=cs: e.dma_start(
            out=xplb[0][:].rearrange("p (lt d) -> p lt d", lt=LT),
            in_=hb["x0"][cs, :].rearrange("(lt p) d -> p lt d", p=128),
            accum_op=mybir.AluOpType.add),
            waits=[(s_xpll, 16 * (s + 1))],
            incs=[(s_xza, 16)])
        for et in range(ET):
            j = s * ET + et
            _full = globals().get("P2_STAGE", 99) >= 3
            k.op("sync", lambda e, et=et, cs=cs, j=j: e.dma_start(
                out=xcld[j % 3][:], in_=hb["xc_s"][et * 128:(et + 1) * 128, cs]),
                waits=[(s_yf1, j - 2)] if j >= 3 and _full else [], incs=[(s_xcl, 16)])
            k.op("sync", lambda e, et=et, cs=cs, j=j: e.dma_start(
                out=ssld[j % 3][:], in_=hb["ss_s"][et * 128:(et + 1) * 128, cs]),
                waits=[(s_yf2, j - 2)] if j >= 3 and _full else [], incs=[(s_ssl, 16)])
        def emit_d2(et):
            j = s * ET + et
            pd = ps_d2[et % 2]
            k.op("tensor", lambda e, et=et, pd=pd, s=s: e.matmul(
                pd[:], wd2T[:, et * 128:(et + 1) * 128], d1b[s % 2][:],
                start=True, stop=True),
                waits=[(s_d1l, 16 * (s + 1))] +
                      ([(s_w, wN)] if j == 0 else []) +
                      ([(s_sp_rd, j - 1)] if j >= 2 else []),
                incs=[(s_mm_d2, 1)])
        emit_d2(0)
        for et in range(ET):
            j = s * ET + et
            pd = ps_d2[et % 2]
            if et + 1 < ET:
                emit_d2(et + 1)
            r_, a_, q_, p_ = rr[et % 2], aa[0], qq[0], pp[et % 2]
            k.op("scalar", lambda e, pd=pd, et=et, r_=r_: e.activation(
                out=r_[:], in_=pd[:], func=AF.Relu, bias=wd2b[:, et:et + 1]),
                waits=[(s_mm_d2, j + 1)] + ([(s_del, j - 1)] if j >= 2 else []))
            k.op("scalar", lambda e, pd=pd, et=et, a_=a_: e.activation(
                out=a_[:], in_=pd[:], func=AF.Abs, bias=wd2b[:, et:et + 1]),
                incs=[(s_sp_rd, 1)])
            k.op("scalar", lambda e, a_=a_, q_=q_: e.activation(
                out=q_[:], in_=a_[:], func=AF.Exp, scale=-1.0))
            k.op("scalar", lambda e, q_=q_, p_=p_: e.activation(
                out=p_[:], in_=q_[:], func=AF.Ln, bias=1.0), incs=[(s_sp, 1)])
            dt_, dx_ = dtl[et % 2], dxl[et % 2]
            k.op("gpsimd", lambda e, r_=r_, p_=p_, dt_=dt_: e.tensor_tensor(
                out=dt_[:], in0=r_[:], in1=p_[:], op=OP.add),
                waits=[(s_sp, j + 1)] +
                      ([(s_abar, N * (j - 1))] if j >= 2 and
                       globals().get("P2_STAGE", 99) >= 2 else []),
                incs=[(s_del, 1)])
            k.op("gpsimd", lambda e, dt_=dt_, dx_=dx_, j=j: e.tensor_tensor(
                out=dx_[:], in0=dt_[:], in1=xcld[j % 3][:], op=OP.mult),
                waits=[(s_xcl, 16 * (j + 1))],
                incs=[(s_dx, 1)])
            if globals().get("P2_STAGE", 99) < 2:
                continue
            for n in range(N):
                i = j * N + n
                ab, ub, hc = abar[i % 4], ubuf[i % 4], hcbuf[i % 4]
                k.op("scalar", lambda e, dt_=dt_, et=et, n=n, ab=ab: e.activation(
                    out=ab[:], in_=dt_[:], func=AF.Exp,
                    scale=negA[:, et * N + n:et * N + n + 1]),
                    waits=[(s_del, j + 1)] +
                          ([(s_negA, 2)] if i == 0 else []) +
                          ([(s_scan, i - 3)] if i >= 4 else []),
                    incs=[(s_abar, 1)])
                k.op("gpsimd", lambda e, dx_=dx_, n=n, ub=ub: e.tensor_tensor(
                    out=ub[:], in0=dx_[:], in1=brep[:, n * CH:(n + 1) * CH],
                    op=OP.mult),
                    waits=[(s_dx, j + 1), (s_br, 16 * (16 * s + n + 1))] +
                          ([(s_scan, i - 3)] if i >= 4 else []),
                    incs=[(s_u, 1)])
                k.op("vector", lambda e, ab=ab, ub=ub, n=n, et=et: e.tensor_tensor_scan(
                    hsl[:, n * CH:(n + 1) * CH], ab[:], ub[:],
                    k.g["state"][:, et * N + n:et * N + n + 1], OP.mult, OP.add),
                    waits=[(s_abar, i + 1), (s_u, i + 1)],
                    incs=[(s_scan, 1)])
                k.op("vector", lambda e, n=n, hc=hc: e.tensor_tensor(
                    out=hc[:], in0=hsl[:, n * CH:(n + 1) * CH],
                    in1=crep[:, n * CH:(n + 1) * CH], op=OP.mult),
                    waits=[(s_cr, 16 * (16 * s + n + 1))] +
                          ([(s_ymm, i - 3)] if i >= 4 else []),
                    incs=[(s_hc, 1)])
                k.op("tensor", lambda e, hc=hc, et=et, n=n: e.matmul(
                    ps_y[et % 2][:], k.g["ident"][:], hc[:],
                    start=(n == 0), stop=(n == N - 1)),
                    waits=[(s_hc, i + 1)] +
                          ([(s_yf1, j - 1)] if n == 0 and j >= 2 and
                           globals().get("P2_STAGE", 99) >= 3 else []),
                    incs=[(s_ymm, 1)])
            k.op("vector", lambda e, et=et: e.tensor_copy(
                k.g["state"][:, et * N:(et + 1) * N],
                hsl[:, CH - 1::CH]),
                )
            if globals().get("P2_STAGE", 99) < 2.5:
                continue
            yb = yf1[et % 2]
            k.op("vector", lambda e, et=et, j=j, yb=yb: e.scalar_tensor_tensor(
                out=yb[:], in0=xcld[j % 3][:], scalar=wdd[:, et:et + 1],
                in1=ps_y[et % 2][:], op0=OP.mult, op1=OP.add),
                waits=[(s_ymm, (j + 1) * N)], incs=[(s_yf1, 1)])
            k.op("vector", lambda e, et=et, j=j, yb=yb: e.tensor_tensor(
                out=yfT[:, et * CH:(et + 1) * CH], in0=yb[:], in1=ssld[j % 3][:],
                op=OP.mult),
                waits=[(s_yf1, j + 1), (s_ssl, 16 * (j + 1))] +
                      ([(s_mm_out, 2 * LT * s)] if et == 0 and s >= 1 and
                       globals().get("P2_STAGE", 99) >= 3 else []),
                incs=[(s_yf2, 1)])

        if s + 1 < NSP:
            emit_bc_batch(s + 1)

        for lt in (range(LT) if globals().get("P2_STAGE", 99) >= 3 else []):
            m = LT * s + lt
            po = ps_o[lt % 2]
            for et in range(ET):
                for half in range(2):
                    k.op("tensor", lambda e, lt=lt, et=et, half=half, po=po: e.matmul(
                        po[:, half * 512:(half + 1) * 512],
                        yfT[:, et * CH + lt * 128:et * CH + (lt + 1) * 128],
                        outT[:, et * D + half * 512:et * D + (half + 1) * 512],
                        start=(et == 0), stop=(et == ET - 1)),
                        waits=([(s_yf2, ET * (s + 1))] if et == 0 and half == 0 else []) +
                              ([(s_x1, m - 1)] if et == 0 and half == 0 and m >= 2 else []),
                        incs=[(s_mm_out, 1)] if et == ET - 1 else [])
            k.op("vector", lambda e, lt=lt, s=s, po=po, m=m: e.scalar_tensor_tensor(
                out=x1b[m % 2][:], in0=po[:], scalar=0.0,
                in1=xplb[0][:, lt * D:(lt + 1) * D], op0=OP.bypass, op1=OP.add),
                waits=[(s_mm_out, 2 * m + 2), (s_xza, 16 * (s + 1))] +
                      ([(s_x1_st, 16 * (m - 1))] if m >= 2 else []),
                incs=[(s_x1, 1)])
    s = NSP - 1
    for lt in (range(LT) if globals().get("P2_STAGE", 99) >= 3 else []):
        m = LT * s + lt
        k.op("sync", lambda e, s=s, lt=lt, m=m: e.dma_start(
            out=hb["rs1_s"][s * CH + lt * 128:s * CH + (lt + 1) * 128, :],
            in_=x1b[m % 2][:]),
            waits=[(s_x1, m + 1)], incs=[(s_x1_st, 16)])
    k.op("sync", lambda e: None, waits=[(s_x1_st, s_x1_st.n)])



def emit_ffn(k, ec, hb, tag, rs_in, rs_out):
    nc = k.nc
    LTF = CHF // 128
    NCHF = L // CHF
    wg = ec(nc.sbuf_tensor(f"{tag}_wg", [128, DT * H], BF16))
    wu = ec(nc.sbuf_tensor(f"{tag}_wu", [128, DT * H], BF16))
    wdt = [ec(nc.sbuf_tensor(f"{tag}_wd{i}", [128, D], BF16)) for i in range(6)]
    rbuf = [ec(nc.sbuf_tensor(f"{tag}_rb{i}", [128, LTF * D], F32)) for i in range(4)]
    xnb = ec(nc.sbuf_tensor(f"{tag}_xnb", [128, LTF * D], BF16))
    xnf32 = ec(nc.sbuf_tensor(f"{tag}_xnf32", [128, LTF * D], F32))
    xnT = [ec(nc.sbuf_tensor(f"{tag}_xnT{i}", [128, DT * CHF], BF16)) for i in range(2)]
    sg = [ec(nc.sbuf_tensor(f"{tag}_sg{i}", [128, CHF], F32)) for i in range(3)]
    hT = [ec(nc.sbuf_tensor(f"{tag}_hT{i}", [128, CHF], BF16)) for i in range(3)]
    ob = [ec(nc.sbuf_tensor(f"{tag}_ob{i}", [128, LTF * D], F32)) for i in range(2)]
    colsA = [ec(nc.sbuf_tensor(f"{tag}_cols{i}", [128, 48], F32)) for i in range(2)]

    ps_g = ec(nc.psum_tensor(f"{tag}_ps_g", [128, 512], F32))
    ps_u = ec(nc.psum_tensor(f"{tag}_ps_u", [128, 512], F32))
    ps_o = [ec(nc.psum_tensor(f"{tag}_ps_o{i}", [128, D], F32)) for i in range(2)]
    ps_tr = ec(nc.psum_tensor(f"{tag}_ps_tr", [128, 2048], BF16))

    s_w = k.fresh(tag + "_w")
    s_wd = k.fresh(tag + "_wd")
    s_res = k.sem(tag + "_res")
    s_rb_done = k.sem(tag + "_rbd")
    s_xn = k.sem(tag + "_xn")
    s_mm_g = k.sem(tag + "_mm_g")
    s_mm_u = k.sem(tag + "_mm_u")
    s_sg = k.sem(tag + "_sg")
    s_hT = k.sem(tag + "_hT")
    s_dn_i = k.sem(tag + "_dn_i")
    s_ob = k.sem(tag + "_ob")
    s_ob_st = k.sem(tag + "_ob_st")

    for dt in range(DT):
        k.op("gpsimd", lambda e, dt=dt: e.dma_start(
            out=wg[:, dt * H:(dt + 1) * H],
            in_=hb["wgT"][dt * 128:(dt + 1) * 128, :]), incs=[(s_w, 16)])
        k.op("gpsimd", lambda e, dt=dt: e.dma_start(
            out=wu[:, dt * H:(dt + 1) * H],
            in_=hb["wuT"][dt * 128:(dt + 1) * 128, :]), incs=[(s_w, 16)])
    wN = s_w.n

    def emit_rb_load(c, w):
        k.op("sync", lambda e, c=c: e.dma_start(
            out=rbuf[c % 4][:].rearrange("p (lt d) -> p lt d", lt=LTF),
            in_=rs_in[c * CHF:(c + 1) * CHF, :].rearrange("(lt p) d -> p lt d", p=128)),
            waits=w, incs=[(s_res, 16)])

    for c in range(min(3, NCHF)):
        emit_rb_load(c, [])
    for c in range(NCHF):
        cs = slice(c * CHF, (c + 1) * CHF)
        rb = rbuf[c % 4]
        xt = xnT[c % 2]
        cols = colsA[c % 2]
        if c + 3 < NCHF:
            emit_rb_load(c + 3, [(s_rb_done, c)])
        for ht in range(HT):
            jw = c * HT + ht
            k.op("gpsimd", lambda e, ht=ht, jw=jw: e.dma_start(
                out=wdt[jw % 6][:], in_=hb["wdT"][ht * 128:(ht + 1) * 128, :]),
                waits=[(s_dn_i, jw - 5)] if jw >= 6 else [],
                incs=[(s_wd, 16)])
        s_rs, s_rb_a = emit_ln_stats(
            k, tag, rb, cols, k.g["sqscr"], c, LTF, [(s_res, 16 * (c + 1))])
        for lt in range(LTF):
            k.op("scalar", lambda e, rb=rb, lt=lt, cols=cols: e.activation(
                out=xnf32[:, lt * D:(lt + 1) * D], in_=rb[:, lt * D:(lt + 1) * D],
                func=AF.Identity, bias=cols[:, 8 + lt:9 + lt]))
            k.op("scalar", lambda e, lt=lt, cols=cols: e.activation(
                out=xnb[:, lt * D:(lt + 1) * D], in_=xnf32[:, lt * D:(lt + 1) * D],
                func=AF.Identity, scale=cols[:, 12 + lt:13 + lt]),
                waits=([(k.sem(tag + "_tr"), DT * c)] if lt == 0 and c >= 1 else []),
                incs=([(s_xn, 1), (s_rb_a, 1)] if lt == LTF - 1 else []))
        s_tr, s_ev = emit_transpose(
            k, tag, xnb, xt, ps_tr, c, LTF,
            [(s_xn, c + 1)] + ([(s_w, wN)] if c == 0 else []))
        def emit_down(ht):
            j = c * HT + ht
            for lt in range(LTF):
                for half in range(2):
                    k.op("tensor", lambda e, ht=ht, lt=lt, half=half, j=j: e.matmul(
                        ps_o[lt][:, half * 512:(half + 1) * 512],
                        hT[j % 3][:, lt * 128:(lt + 1) * 128],
                        wdt[j % 6][:, half * 512:(half + 1) * 512],
                        start=(ht == 0), stop=(ht == HT - 1)),
                        waits=([(s_hT, j + 1), (s_wd, 16 * (j + 1))]
                               if lt == 0 and half == 0 else []) +
                              ([(s_ob, LTF * c)] if ht == 0 and lt == 0 and half == 0
                               and c >= 1 else []),
                        incs=[(s_dn_i, 1)] if lt == LTF - 1 and half == 1 else [])

        for ht in range(HT):
            j = c * HT + ht
            for dt in range(DT):
                k.op("tensor", lambda e, dt=dt, ht=ht, xt=xt: e.matmul(
                    ps_g[:, 0:CHF], wg[:, dt * H + ht * 128:dt * H + (ht + 1) * 128],
                    xt[:, dt * CHF:(dt + 1) * CHF],
                    start=(dt == 0), stop=(dt == DT - 1)),
                    waits=([(s_ev, DT * (c + 1))] if dt == 0 and ht == 0 else []) +
                          ([(s_sg, j)] if dt == 0 and j >= 1 else []),
                    incs=[(s_mm_g, 1)] if dt == DT - 1 else [])
            for dt in range(DT):
                k.op("tensor", lambda e, dt=dt, ht=ht, xt=xt: e.matmul(
                    ps_u[:, 0:CHF], wu[:, dt * H + ht * 128:dt * H + (ht + 1) * 128],
                    xt[:, dt * CHF:(dt + 1) * CHF],
                    start=(dt == 0), stop=(dt == DT - 1)),
                    waits=[(s_hT, j)] if dt == 0 and j >= 1 else [],
                    incs=[(s_mm_u, 1)] if dt == DT - 1 else [])
            k.op("scalar", lambda e, j=j: e.activation(
                out=sg[j % 3][:], in_=ps_g[:, 0:CHF], func=AF.Silu, scale=FFS),
                waits=[(s_mm_g, j + 1)], incs=[(s_sg, 1)])
            k.op("vector", lambda e, j=j: e.scalar_tensor_tensor(
                out=hT[j % 3][:], in0=ps_u[:, 0:CHF], scalar=FFS, in1=sg[j % 3][:],
                op0=OP.mult, op1=OP.mult),
                waits=[(s_mm_u, j + 1), (s_sg, j + 1)], incs=[(s_hT, 1)])
            if ht >= 1:
                emit_down(ht - 1)
        emit_down(HT - 1)
        for lt in range(LTF):
            k.op("vector", lambda e, lt=lt, rb=rb, c=c: e.scalar_tensor_tensor(
                out=ob[c % 2][:, lt * D:(lt + 1) * D], in0=ps_o[lt][:], scalar=FFS,
                in1=rb[:, lt * D:(lt + 1) * D], op0=OP.mult, op1=OP.add),
                waits=[(s_dn_i, HT * (c + 1))] +
                      ([(s_ob_st, 16 * (c - 1))] if lt == 0 and c >= 2 else []),
                incs=[(s_ob, 1)] + ([(s_rb_done, 1)] if lt == LTF - 1 else []))
        k.op("sync", lambda e, c=c, cs=cs: e.dma_start(
            out=rs_out[cs, :].rearrange("(lt p) d -> p lt d", p=128),
            in_=ob[c % 2][:].rearrange("p (lt d) -> p lt d", lt=LTF)),
            waits=[(s_ob, LTF * (c + 1))], incs=[(s_ob_st, 16)])
    k.op("sync", lambda e: None, waits=[(s_ob_st, s_ob_st.n)])



def emit_p4(k, ec, hb):
    nc = k.nc
    LT4 = CH4 // 128
    NC4 = L // CH4
    wk = ec(nc.sbuf_tensor("p4_wk", [128, DT * D], BF16))
    wka = ec(nc.sbuf_tensor("p4_wka", [128, DT * D], BF16))
    wv = ec(nc.sbuf_tensor("p4_wv", [128, DT * D], BF16))
    gcol = ec(nc.sbuf_tensor("p4_g", [128, DT], F32))
    bcol = ec(nc.sbuf_tensor("p4_b", [128, DT], F32))
    bcol_bf = ec(nc.sbuf_tensor("p4_bbf", [128, DT], BF16))
    bkab = ec(nc.sbuf_tensor("p4_bka", [128, 2 * DT], F32))
    bkrow = ec(nc.sbuf_tensor("p4_bkrow", [1, D], F32))
    bvrow = ec(nc.sbuf_tensor("p4_bvrow", [1, D], F32))
    biask = ec(nc.sbuf_tensor("p4_biask", [1, D], BF16))
    biasv = ec(nc.sbuf_tensor("p4_biasv", [1, D], BF16))
    ones_bf = ec(nc.sbuf_tensor("p4_ones", [1, CH4], BF16))
    ones_f = ec(nc.sbuf_tensor("p4_onesf", [128, CH4], F32))
    rbuf = [ec(nc.sbuf_tensor(f"p4_rb{i}", [128, LT4 * D], F32)) for i in range(2)]
    xnb = ec(nc.sbuf_tensor("p4_xnb", [128, LT4 * D], BF16))
    xnf32 = ec(nc.sbuf_tensor("p4_xnf32", [128, LT4 * D], F32))
    xnT = [ec(nc.sbuf_tensor(f"p4_xnT{i}", [128, DT * CH4], BF16)) for i in range(2)]
    keysT = [ec(nc.sbuf_tensor(f"p4_keys{i}", [128, DT * CH4], F32)) for i in range(2)]
    keysbf = ec(nc.sbuf_tensor("p4_keysbf", [128, DT * CH4], BF16))
    valsT = [ec(nc.sbuf_tensor(f"p4_vals{i}", [128, DT * CH4], F32)) for i in range(2)]
    kwT = ec(nc.sbuf_tensor("p4_kwT", [128, DT * CH4], F32))
    sT = ec(nc.sbuf_tensor("p4_sT", [128, DT * CH4], F32))
    ivT = ec(nc.sbuf_tensor("p4_ivT", [128, DT * CH4], F32))
    t1T = ec(nc.sbuf_tensor("p4_t1T", [128, DT * CH4], F32))
    kkT = ec(nc.sbuf_tensor("p4_kkT", [128, DT * CH4], F32))
    pT = ec(nc.sbuf_tensor("p4_pT", [128, DT * CH4], F32))
    x3b = [ec(nc.sbuf_tensor(f"p4_x3{i}", [128, LT4 * D], F32)) for i in range(2)]
    colsA = [ec(nc.sbuf_tensor(f"p4_cols{i}", [128, 48], F32)) for i in range(2)]

    ps_k = [ec(nc.psum_tensor(f"p4_ps_k{i}", [128, 512], F32)) for i in range(2)]
    ps_ka = ec(nc.psum_tensor("p4_ps_ka", [128, 512], F32))
    ps_v = ec(nc.psum_tensor("p4_ps_v", [128, 512], F32))
    ps_ot = ec(nc.psum_tensor("p4_ps_ot", [128, D], F32))
    ps_tr = ec(nc.psum_tensor("p4_ps_tr", [128, 2048], BF16))

    s_w = k.fresh("p4_w")
    s_wv = k.sem("p4_wv")
    s_fold = k.sem("p4_fold")
    s_bias = k.sem("p4_bias")
    s_res = k.sem("p4_res")
    s_rb_v = k.sem("p4_rb_v")
    s_xn = k.sem("p4_xn")
    s_mm_k = k.sem("p4_mm_k")
    s_kev = k.sem("p4_kev")
    s_kbf = k.sem("p4_kbf")
    s_mm_ka = k.sem("p4_mm_ka")
    s_kw = k.sem("p4_kw")
    s_mm_v = k.sem("p4_mm_v")
    s_vev = k.sem("p4_vev")
    s_scs = k.sem("p4_scs")
    s_lnS = k.sem("p4_lnS")
    s_t1 = k.sem("p4_t1")
    s_kk = k.sem("p4_kk")
    s_pcs = k.sem("p4_pcs")
    s_at = k.sem("p4_at")
    s_otr = k.sem("p4_otr")
    s_x3 = k.sem("p4_x3")
    s_x3_st = k.sem("p4_x3_st")

    for dt in range(DT):
        k.op("gpsimd", lambda e, dt=dt: e.dma_start(
            out=wk[:, dt * D:(dt + 1) * D],
            in_=hb["wkT"][dt * 128:(dt + 1) * 128, :]), incs=[(s_w, 16)])
        k.op("gpsimd", lambda e, dt=dt: e.dma_start(
            out=wka[:, dt * D:(dt + 1) * D],
            in_=hb["wkaT"][dt * 128:(dt + 1) * 128, :]), incs=[(s_w, 16)])
        k.op("gpsimd", lambda e, dt=dt: e.dma_start(
            out=wv[:, dt * D:(dt + 1) * D],
            in_=hb["wvT"][dt * 128:(dt + 1) * 128, :]), incs=[(s_w, 16)])
    for nm, t in (("gamma_col", gcol), ("beta_col", bcol),
                  ("bk_row", bkrow), ("bv_row", bvrow)):
        k.op("sync", lambda e, nm=nm, t=t: e.dma_start(out=t[:], in_=hb[nm][:]),
             incs=[(s_wv, 16)])
    k.op("sync", lambda e: e.dma_start(out=bkab[:, 0:DT], in_=hb["bka_col"][:]),
         incs=[(s_wv, 16)])
    wN, wvN = s_w.n, s_wv.n
    k.op("vector", lambda e: e.memset(ones_bf[:], 1.0))
    k.op("vector", lambda e: e.memset(ones_f[:], 1.0))
    k.op("vector", lambda e: e.tensor_copy(bcol_bf[:], bcol[:]),
         waits=[(s_wv, wvN)], incs=[(s_fold, 1)])
    k.op("scalar", lambda e: e.activation(
        out=bkab[:, DT:2 * DT], in_=bkab[:, 0:DT], func=AF.Identity,
        scale=1.0 / 32.0),
        waits=[(s_wv, wvN)],
        incs=[(s_fold, 1)])
    foldN = s_fold.n
    ps_bias = ps_ka
    rounds = [(wk, bkrow, biask, 0), (wk, bkrow, biask, 1),
              (wv, bvrow, biasv, 0), (wv, bvrow, biasv, 1)]
    for ri, (wt, brow, bout, half) in enumerate(rounds):
        for dt in range(DT):
            k.op("tensor", lambda e, wt=wt, half=half, dt=dt: e.matmul(
                ps_bias[0:1, 0:512],
                bcol_bf[:, dt:dt + 1],
                wt[:, dt * D + half * 512:dt * D + (half + 1) * 512],
                start=(dt == 0), stop=(dt == DT - 1)),
                waits=([(s_fold, foldN), (s_w, wN)] if ri == 0 and dt == 0
                       else []) +
                      ([(s_bias, 2 * ri)] if ri > 0 and dt == 0 else []),
                incs=[(s_bias, 1)] if dt == DT - 1 else [])
        tb = s_bias.n
        k.op("vector", lambda e, brow=brow, bout=bout, half=half: e.tensor_tensor(
            out=bout[0:1, half * 512:(half + 1) * 512], in0=ps_bias[0:1, 0:512],
            in1=brow[0:1, half * 512:(half + 1) * 512], op=OP.add),
            waits=[(s_bias, tb)], incs=[(s_bias, 1)])
    biasN = s_bias.n
    wstage4 = ec(nc.sbuf_tensor("p4_wstage", [128, D], BF16))
    s_gf = k.sem("p4_gf")
    for dt in range(DT):
        k.op("vector", lambda e, dt=dt: e.tensor_copy(
            wstage4[:], wk[:, dt * D:(dt + 1) * D]),
            waits=[(s_bias, biasN)] if dt == 0 else [])
        k.op("vector", lambda e, dt=dt: e.tensor_scalar(
            out=wk[:, dt * D:(dt + 1) * D], in0=wstage4[:],
            scalar1=gcol[:, dt:dt + 1], scalar2=None, op0=OP.mult))
        k.op("vector", lambda e, dt=dt: e.tensor_copy(
            wstage4[:], wv[:, dt * D:(dt + 1) * D]))
        k.op("vector", lambda e, dt=dt: e.tensor_scalar(
            out=wv[:, dt * D:(dt + 1) * D], in0=wstage4[:],
            scalar1=gcol[:, dt:dt + 1], scalar2=None, op0=OP.mult),
            incs=[(s_gf, 1)] if dt == DT - 1 else [])
    gfN = s_gf.n

    for c in range(NC4):
        cs = slice(c * CH4, (c + 1) * CH4)
        rb = rbuf[c % 2]
        xt = xnT[c % 2]
        kt = keysT[c % 2]
        vt = valsT[c % 2]
        cols = colsA[c % 2]
        k.op("sync", lambda e, rb=rb, cs=cs: e.dma_start(
            out=rb[:].rearrange("p (lt d) -> p lt d", lt=LT4),
            in_=hb["rs2_s"][cs, :].rearrange("(lt p) d -> p lt d", p=128)),
            waits=([(s_rb_v, c - 1)] if c >= 2 and globals().get("P4_STAGE", 99) >= 4 else []),
            incs=[(s_res, 16)])
        s_rs, s_rb_a = emit_ln_stats(
            k, "p4", rb, cols, k.g["sqscr"], c, LT4, [(s_res, 16 * (c + 1))])
        for lt in range(LT4):
            k.op("scalar", lambda e, rb=rb, lt=lt, cols=cols: e.activation(
                out=xnf32[:, lt * D:(lt + 1) * D], in_=rb[:, lt * D:(lt + 1) * D],
                func=AF.Identity, bias=cols[:, 8 + lt:9 + lt]))
            k.op("scalar", lambda e, lt=lt, cols=cols: e.activation(
                out=xnf32[:, lt * D:(lt + 1) * D],
                in_=xnf32[:, lt * D:(lt + 1) * D],
                func=AF.Identity, scale=cols[:, 12 + lt:13 + lt]))
        for lt in range(LT4):
            k.op("scalar", lambda e, lt=lt, cols=cols: e.activation(
                out=k.g["sqscr"][:], in_=xnf32[:, lt * D:(lt + 1) * D],
                func=AF.Square, accum_out=cols[:, 32 + lt:33 + lt]))
        k.op("scalar", lambda e, cols=cols: e.activation(
            out=cols[:, 36:36 + LT4], in_=cols[:, 32:32 + LT4], func=AF.Ln,
            scale=1.0 / D, bias=EPS))
        k.op("scalar", lambda e, cols=cols: e.activation(
            out=cols[:, 36:36 + LT4], in_=cols[:, 36:36 + LT4], func=AF.Exp,
            scale=-0.5))
        for lt in range(LT4):
            k.op("scalar", lambda e, lt=lt, cols=cols: e.activation(
                out=xnb[:, lt * D:(lt + 1) * D], in_=xnf32[:, lt * D:(lt + 1) * D],
                func=AF.Identity, scale=cols[:, 36 + lt:37 + lt]),
                waits=([(k.sem("p4_tr"), DT * c)] if lt == 0 and c >= 1 else []),
                incs=([(s_xn, 1)] if lt == LT4 - 1 else []))
        s_tr, s_ev = emit_transpose(
            k, "p4", xnb, xt, ps_tr, c, LT4,
            [(s_xn, c + 1)] + ([(s_bias, biasN)] if c == 0 else []))
        if globals().get("P4_STAGE", 99) < 2:
            continue
        for dt2 in range(DT):
            j = c * DT + dt2
            pk = ps_k[j % 2]
            for dt in range(DT):
                k.op("tensor", lambda e, dt=dt, dt2=dt2, xt=xt, pk=pk: e.matmul(
                    pk[:, 0:CH4], wk[:, dt * D + dt2 * 128:dt * D + (dt2 + 1) * 128],
                    xt[:, dt * CH4:(dt + 1) * CH4],
                    start=(dt == 0), stop=False),
                    waits=([(s_ev, DT * (c + 1))] if dt == 0 and dt2 == 0 else []) +
                          ([(s_gf, gfN)] if c == 0 and dt == 0 and dt2 == 0
                           else []) +
                          ([(s_kev, j - 1)] if dt == 0 and j >= 2 else []))
            k.op("tensor", lambda e, dt2=dt2, pk=pk: e.matmul(
                pk[:, 0:CH4], biask[0:1, dt2 * 128:(dt2 + 1) * 128],
                ones_bf[0:1, :], start=False, stop=True), incs=[(s_mm_k, 1)])
            k.op("scalar", lambda e, dt2=dt2, kt=kt, pk=pk: e.activation(
                out=kt[:, dt2 * CH4:(dt2 + 1) * CH4], in_=pk[:, 0:CH4],
                func=AF.Identity),
                waits=[(s_mm_k, j + 1)] + ([(s_t1, 2 * (c - 1))] if dt2 == 0 and c >= 2 and globals().get("P4_STAGE", 99) >= 3 else []),
                incs=[(s_kev, 1)])
            k.op("vector", lambda e, dt2=dt2, kt=kt: e.tensor_copy(
                keysbf[:, dt2 * CH4:(dt2 + 1) * CH4],
                kt[:, dt2 * CH4:(dt2 + 1) * CH4]),
                waits=[(s_kev, j + 1)] +
                      ([(s_mm_ka, DT * c)] if dt2 == 0 and c >= 1 else []),
                incs=[(s_kbf, 1)])
        for dt2 in range(DT):
            j = c * DT + dt2
            for dt in range(DT):
                k.op("tensor", lambda e, dt=dt, dt2=dt2, xt=xt: e.matmul(
                    ps_v[:, 0:CH4], wv[:, dt * D + dt2 * 128:dt * D + (dt2 + 1) * 128],
                    xt[:, dt * CH4:(dt + 1) * CH4],
                    start=(dt == 0), stop=False),
                    waits=[(s_vev, j)] if dt == 0 and j >= 1 else [])
            k.op("tensor", lambda e, dt2=dt2: e.matmul(
                ps_v[:, 0:CH4], biasv[0:1, dt2 * 128:(dt2 + 1) * 128],
                ones_bf[0:1, :], start=False, stop=True), incs=[(s_mm_v, 1)])
            k.op("scalar", lambda e, dt2=dt2, vt=vt: e.activation(
                out=vt[:, dt2 * CH4:(dt2 + 1) * CH4], in_=ps_v[:, 0:CH4],
                func=AF.Identity),
                waits=[(s_mm_v, j + 1)] + ([(s_at, 2 * (c - 1))] if dt2 == 0 and c >= 2 and globals().get("P4_STAGE", 99) >= 4 else []),
                incs=[(s_vev, 1)])
        for dt2 in range(DT):
            j = c * DT + dt2
            for dt in range(DT):
                k.op("tensor", lambda e, dt=dt, dt2=dt2: e.matmul(
                    ps_ka[:, 0:CH4],
                    wka[:, dt * D + dt2 * 128:dt * D + (dt2 + 1) * 128],
                    keysbf[:, dt * CH4:(dt + 1) * CH4],
                    start=(dt == 0), stop=(dt == DT - 1)),
                    waits=([(s_kbf, DT * (c + 1))] if dt == 0 and dt2 == 0 else []) +
                          ([(s_kw, j)] if dt == 0 and j >= 1 else []),
                    incs=[(s_mm_ka, 1)] if dt == DT - 1 else [])
            k.op("scalar", lambda e, dt2=dt2: e.activation(
                out=kwT[:, dt2 * CH4:(dt2 + 1) * CH4], in_=ps_ka[:, 0:CH4],
                func=AF.Exp, scale=1.0 / 32.0, bias=bkab[:, dt2:dt2 + 1]),
                waits=[(s_mm_ka, j + 1)] +
                      (([(s_scs, DT * c), (s_t1, 2 * c)] if globals().get("P4_STAGE", 99) >= 3 else []) +
                       ([(s_otr, DT * LT4 * c)] if globals().get("P4_STAGE", 99) >= 4 else [])
                       if dt2 == 0 and c >= 1 else []),
                incs=[(s_kw, 1)])
        if globals().get("P4_STAGE", 99) < 3:
            continue
        HW4 = (DT // 2) * CH4
        for dt2 in range(DT):
            k.op("vector", lambda e, dt2=dt2: e.tensor_tensor_scan(
                sT[:, dt2 * CH4:(dt2 + 1) * CH4], ones_f[:, 0:CH4],
                kwT[:, dt2 * CH4:(dt2 + 1) * CH4],
                k.g["sstate"][:, dt2:dt2 + 1], OP.mult, OP.add),
                waits=[(s_kw, c * DT + dt2 + 1)] +
                      ([(s_lnS, 2 * (c - 1) + dt2 // 4 + 1)] if c >= 1 else []),
                incs=[(s_scs, 1)])
        k.op("vector", lambda e: e.tensor_copy(
            k.g["sstate"][:, 0:DT], sT[:, CH4 - 1::CH4]))
        for h in range(2):
            hs = slice(h * HW4, (h + 1) * HW4)
            k.op("scalar", lambda e, hs=hs: e.activation(
                out=pT[:, hs], in_=sT[:, hs], func=AF.Ln),
                waits=[(s_scs, c * DT + 4 * (h + 1))] +
                      ([(s_at, 2 * (c - 1) + h + 1)] if c >= 1 and globals().get("P4_STAGE", 99) >= 4 else []))
            k.op("scalar", lambda e, hs=hs: e.activation(
                out=ivT[:, hs], in_=pT[:, hs], func=AF.Exp, scale=-1.0),
                waits=[(s_kk, 2 * (c - 1) + h + 1)] if c >= 1 else [],
                incs=[(s_lnS, 1)])
        for h in range(2):
            hs = slice(h * HW4, (h + 1) * HW4)
            k.op("gpsimd", lambda e, kt=kt, hs=hs: e.tensor_tensor(
                out=t1T[:, hs], in0=kwT[:, hs], in1=kt[:, hs], op=OP.mult),
                waits=[(s_kw, c * DT + 4 * (h + 1)), (s_kev, c * DT + 4 * (h + 1))] +
                      ([(s_kk, 2 * (c - 1) + h + 1)] if c >= 1 else []),
                incs=[(s_t1, 1)])
        for h in range(2):
            hs = slice(h * HW4, (h + 1) * HW4)
            k.op("gpsimd", lambda e, hs=hs: e.tensor_tensor(
                out=kkT[:, hs], in0=t1T[:, hs], in1=ivT[:, hs], op=OP.mult),
                waits=[(s_t1, 2 * c + h + 1), (s_lnS, 2 * c + h + 1)] +
                      ([(s_pcs, (c - 1) * DT + 4 * (h + 1))] if c >= 1 else []),
                incs=[(s_kk, 1)])
        for dt2 in range(DT):
            k.op("vector", lambda e, dt2=dt2: e.tensor_tensor_scan(
                pT[:, dt2 * CH4:(dt2 + 1) * CH4], ones_f[:, 0:CH4],
                kkT[:, dt2 * CH4:(dt2 + 1) * CH4],
                k.g["pstate"][:, dt2:dt2 + 1], OP.mult, OP.add),
                waits=[(s_kk, 2 * c + dt2 // 4 + 1)] +
                      ([(s_at, 2 * (c - 1) + dt2 // 4 + 1)] if c >= 1 and globals().get("P4_STAGE", 99) >= 4 else []),
                incs=[(s_pcs, 1)])
        k.op("vector", lambda e: e.tensor_copy(
            k.g["pstate"][:, 0:DT], pT[:, CH4 - 1::CH4]))
        if globals().get("P4_STAGE", 99) < 4:
            continue
        for h in range(2):
            hs = slice(h * HW4, (h + 1) * HW4)
            k.op("gpsimd", lambda e, vt=vt, hs=hs: e.tensor_tensor(
                out=kwT[:, hs], in0=pT[:, hs], in1=vt[:, hs], op=OP.mult),
                waits=[(s_pcs, c * DT + 4 * (h + 1)), (s_vev, c * DT + 4 * (h + 1))] +
                      ([(s_otr, DT * LT4 * c)] if c >= 1 else []),
                incs=[(s_at, 1)])
        for lt in range(LT4):
            m = LT4 * c + lt
            for dt in range(DT):
                k.op("tensor", lambda e, lt=lt, dt=dt: e.transpose(
                    ps_ot[:, dt * 128:(dt + 1) * 128],
                    kwT[:, dt * CH4 + lt * 128:dt * CH4 + (lt + 1) * 128],
                    k.g["identf"][:]),
                    waits=([(s_at, 2 * c + dt // 4 + 1)] if dt in (0, 4) else []) +
                          ([(s_x3, m)] if dt == 0 and m >= 1 else []),
                    incs=[(s_otr, 1)])
            k.op("vector", lambda e, lt=lt, rb=rb, c=c: e.scalar_tensor_tensor(
                out=x3b[c % 2][:, lt * D:(lt + 1) * D], in0=ps_ot[:], scalar=0.0,
                in1=rb[:, lt * D:(lt + 1) * D], op0=OP.bypass, op1=OP.add),
                waits=[(s_otr, DT * (m + 1))],
                incs=[(s_x3, 1)] + ([(s_rb_v, 1)] if lt == LT4 - 1 else []))
        k.op("sync", lambda e, c=c, cs=cs: e.dma_start(
            out=hb["rs3_s"][cs, :].rearrange("(lt p) d -> p lt d", p=128),
            in_=x3b[c % 2][:].rearrange("p (lt d) -> p lt d", lt=LT4)),
            waits=[(s_x3, LT4 * (c + 1))] +
                  ([(s_x3_st, 16 * (c - 1))] if c >= 2 else []),
            incs=[(s_x3_st, 16)])
    k.op("sync", lambda e: None, waits=[(s_x3_st, s_x3_st.n)])



def build_nc(passes=_BUILD_PASSES, debug=()):
    nc = bass.Bass("TRN2", target_bir_lowering=False,
                   detect_race_conditions=False)
    dbg = set(debug)

    hb = {}
    hb["x0"] = nc.dram_tensor("x0", [L, D], F32, kind="ExternalInput").ap()
    for nm, shape, dt in (
        ("skipT", [D, E], F32), ("inT", [D, E], F32), ("outT", [E, D], F32),
        ("wd1T", [E, DDL], F32), ("wd2T", [DDL, E], F32),
        ("wbT", [E, N], F32), ("wcT", [E, N], F32),
        ("normw_col", [128, DT], F32), ("convw_col", [128, ET * 4], F32),
        ("convb_col", [128, ET], F32), ("wd2b_col", [128, ET], F32),
        ("wD_col", [128, ET], F32), ("alog_col", [128, ET * N], F32),
        ("wgT", [D, H], I32), ("wuT", [D, H], I32), ("wdT", [H, D], I32),
        ("wkT", [D, D], F32), ("wkaT", [D, D], F32), ("wvT", [D, D], F32),
        ("gamma_col", [128, DT], F32), ("beta_col", [128, DT], F32),
        ("bka_col", [128, DT], F32), ("bk_row", [1, D], F32), ("bv_row", [1, D], F32),
        ("ident", [128, 128], BF16), ("identf", [128, 128], F32),
    ):
        hb[nm] = nc.dram_tensor(nm, shape, dt, kind="ExternalInput").ap()
    hb["out"] = nc.dram_tensor("out", [L, D], F32, kind="ExternalOutput").ap()

    def scratch(nm, shape, dt):
        kind = "ExternalOutput" if nm in dbg else "Internal"
        hb[nm] = nc.dram_tensor(nm, shape, dt, kind=kind).ap()
    scratch("xpl_s", [L, D], F32)
    scratch("xc_s", [E, L], F32)
    scratch("ss_s", [E, L], F32)
    scratch("d1_s", [DDL, L], BF16)
    scratch("bm_s", [N, L], F32)
    scratch("cm_s", [N, L], F32)
    scratch("bmb_s", [N, L], BF16)
    scratch("cmb_s", [N, L], BF16)
    scratch("rs1_s", [L, D], F32)
    scratch("rs2_s", [L, D], F32)
    scratch("rs3_s", [L, D], F32)

    stack = ExitStack()
    with stack:
        k = K(nc, stack)
        ec = stack.enter_context
        k.g["ident"] = ec(nc.sbuf_tensor("identg", [128, 128], BF16))
        k.g["identf"] = ec(nc.sbuf_tensor("identfg", [128, 128], F32))
        k.g["state"] = ec(nc.sbuf_tensor("stateg", [128, ET * N], F32))
        k.g["sstate"] = ec(nc.sbuf_tensor("sstateg", [128, DT], F32))
        k.g["pstate"] = ec(nc.sbuf_tensor("pstateg", [128, DT], F32))
        k.g["sqscr"] = ec(nc.sbuf_tensor("sqscrg", [128, D], F32))

        CONSTS = [EPS, 1.0, -0.5, -1.0, 1.0 / D, -1.0 / D, 1.0 / 32.0, FFS]
        constc = ec(nc.sbuf_tensor("constc", [128, len(CONSTS)], F32))
        s0 = k.sem("init")
        for i, v in enumerate(CONSTS):
            nc.const_aps.register(F32, v, constc[:, i:i + 1])
            k.op("vector", lambda e, i=i, v=v: e.memset(constc[:, i:i + 1], v),
                 incs=[(s0, 1)])
        k.op("sync", lambda e: e.dma_start(out=k.g["ident"][:], in_=hb["ident"][:]),
             incs=[(s0, 16)])
        k.op("sync", lambda e: e.dma_start(out=k.g["identf"][:], in_=hb["identf"][:]),
             incs=[(s0, 16)])
        k.op("vector", lambda e: e.memset(k.g["state"][:], 0.0), incs=[(s0, 1)])
        k.op("vector", lambda e: e.memset(k.g["sstate"][:], 0.0), incs=[(s0, 1)])
        k.op("vector", lambda e: e.memset(k.g["pstate"][:], 0.0), incs=[(s0, 1)])
        t0 = s0.n
        for eng in ENGS:
            k.op(eng, lambda e: None, waits=[(s0, t0)])

        if 1 in passes:
            k.begin_pass()
            with ExitStack() as p:
                emit_p1(k, p.enter_context, hb)
            k.end_pass()
        if 2 in passes:
            k.begin_pass()
            with ExitStack() as p:
                emit_p2(k, p.enter_context, hb)
            k.end_pass()
        if 3 in passes:
            k.begin_pass()
            with ExitStack() as p:
                emit_ffn(k, p.enter_context, hb, "f1", hb["rs1_s"], hb["rs2_s"])
            k.end_pass()
        if 4 in passes:
            k.begin_pass()
            with ExitStack() as p:
                emit_p4(k, p.enter_context, hb)
            k.end_pass()
        if 5 in passes:
            k.begin_pass()
            with ExitStack() as p:
                emit_ffn(k, p.enter_context, hb, "f2", hb["rs3_s"], hb["out"])
            k.end_pass()

        with nc.Block() as block:
            @block.sync
            def _(e):
                for fn in k.plan["sync"]:
                    fn(e)

            @block.vector
            def _(e):
                for fn in k.plan["vector"]:
                    fn(e)

            @block.scalar
            def _(e):
                for fn in k.plan["scalar"]:
                    fn(e)

            @block.tensor
            def _(e):
                for fn in k.plan["tensor"]:
                    fn(e)

            @block.gpsimd
            def _(e):
                for fn in k.plan["gpsimd"]:
                    fn(e)
    return nc



def _prep_inputs(inputs):
    t = lambda a: np.ascontiguousarray(np.asarray(a).T)
    col = lambda v, kk: np.ascontiguousarray(np.asarray(v).reshape(kk, 128).T)
    g = {}
    g["skipT"] = t(inputs["L0_attn_a_skip_w"])
    g["inT"] = t(inputs["L0_attn_a_in_w"])
    g["outT"] = t(inputs["L0_attn_a_out_w"])
    g["wd1T"] = t(inputs["L0_attn_a_Wd1"])
    g["wd2T"] = t(inputs["L0_attn_a_Wd2_w"])
    g["wbT"] = t(inputs["L0_attn_a_WB"])
    g["wcT"] = t(inputs["L0_attn_a_WC"])
    g["normw_col"] = col(inputs["L0_attn_a_norm_w"], DT)
    g["convb_col"] = col(inputs["L0_attn_a_conv_b"], ET)
    g["wd2b_col"] = col(inputs["L0_attn_a_Wd2_b"], ET)
    g["wD_col"] = col(inputs["L0_attn_a_W_D"], ET)
    cw = np.asarray(inputs["L0_attn_a_conv_w"])[:, 0, :]
    g["convw_col"] = np.ascontiguousarray(
        cw.reshape(ET, 128, 4).transpose(1, 0, 2).reshape(128, ET * 4))
    al = np.asarray(inputs["L0_attn_a_A_log"])
    g["alog_col"] = np.ascontiguousarray(
        al.reshape(ET, 128, N).transpose(1, 0, 2).reshape(128, ET * N))
    g["wgT"] = t(inputs["L0_ffn_w_gate"])
    g["wuT"] = t(inputs["L0_ffn_w_up"])
    g["wdT"] = t(inputs["L0_ffn_w_down"])
    g["wkT"] = t(inputs["L0_attn_b_Wk"])
    g["wkaT"] = t(inputs["L0_attn_b_Wka"])
    g["wvT"] = t(inputs["L0_attn_b_Wv"])
    g["gamma_col"] = col(inputs["L0_attn_b_gamma"], DT)
    g["beta_col"] = col(inputs["L0_attn_b_beta"], DT)
    g["bka_col"] = col(inputs["L0_attn_b_bka"], DT)
    g["bk_row"] = np.ascontiguousarray(
        np.asarray(inputs["L0_attn_b_bk"]).reshape(1, D))
    g["bv_row"] = np.ascontiguousarray(
        np.asarray(inputs["L0_attn_b_bv"]).reshape(1, D))
    g["ident"] = np.eye(128, dtype=np.float32).astype(ml_dtypes.bfloat16)
    g["identf"] = np.eye(128, dtype=np.float32)
    return g


def kernel(**inputs):
    resid = np.asarray(inputs["L0_attn_a_resid"])
    g = _prep_inputs(inputs)
    nc = build_nc()
    in_maps = []
    for b in range(8):
        m = dict(g)
        m["x0"] = np.ascontiguousarray(resid[b])
        in_maps.append(m)
    res = run_bass_kernel_spmd(nc, in_maps, core_ids=list(range(8)))
    return np.stack([np.asarray(res.results[b]["out"]) for b in range(8)], axis=0)
